# version 16
# baseline (speedup 1.0000x reference)
import numpy as np
import concourse.bass as bass
import concourse.mybir as mybir
from concourse.bass_utils import run_bass_kernel_spmd
from contextlib import ExitStack

F32 = mybir.dt.float32
BF16 = mybir.dt.bfloat16
AF = mybir.ActivationFunctionType
ALU = mybir.AluOpType
AX = mybir.AxisListType

D = 4096
T = 2048
NDT = 32
NH = 32
QC = 256
NQC = T // QC
GPH = 36
EPS = 1e-5
ISQ = 0.08838834764831845
NEGBIG = -1.0e30

_STOP_PHASE = None
_DEV_NH = None


class Ctr:
    def __init__(self, st, nc, name):
        self.sem = st.enter_context(nc.semaphore(name))
        self.n = 0

    def inc(self, ins, k=1):
        ins.then_inc(self.sem, k)
        self.n += k
        return self.n


def _chain(eng, ctr, ins):
    ctr.inc(ins)
    eng.wait_ge(ctr.sem, ctr.n)


def _getc(ctrs, gst, nc, name):
    if name not in ctrs:
        ctrs[name] = Ctr(gst, nc, name)
    return ctrs[name]


def host_pack(inputs):
    x = np.asarray(inputs["L0_attn_x"], np.float32)
    Wq = np.asarray(inputs["L0_attn_Wq"], np.float32)
    Wk = np.asarray(inputs["L0_attn_Wk"], np.float32)
    Wv = np.asarray(inputs["L0_attn_Wv"], np.float32)
    Wo = np.asarray(inputs["L0_attn_Wo"], np.float32)
    bq = np.asarray(inputs["L0_attn_bq"], np.float32)
    bk = np.asarray(inputs["L0_attn_bk"], np.float32)
    bv = np.asarray(inputs["L0_attn_bv"], np.float32)
    bo = np.asarray(inputs["L0_attn_bo"], np.float32)
    rw = np.asarray(inputs["L0_ffn_router_w"], np.float32)
    A = np.asarray(inputs["L0_ffn_A"], np.float32)
    B = np.asarray(inputs["L0_ffn_B"], np.float32)

    xT = np.ascontiguousarray(x.transpose(0, 2, 1))

    def packw(W):
        return W.reshape(NH, 128, NDT, 128).transpose(0, 3, 2, 1)

    wqkv = np.ascontiguousarray(
        np.stack([packw(Wq), packw(Wk), packw(Wv)], axis=1)
    ).reshape(NH, 3, 128, NDT * 128)
    wo = np.ascontiguousarray(packw(Wo)).reshape(NDT, 128, NDT * 128)

    art = np.concatenate([A.reshape(64, D).T, rw.T], axis=1)
    art = np.ascontiguousarray(
        art.reshape(NDT, 128, 68).transpose(1, 0, 2)).reshape(128, NDT * 68)
    bt = np.ascontiguousarray(B.transpose(0, 2, 1)).reshape(64, D)

    bqkv = np.ascontiguousarray(np.concatenate(
        [b.reshape(NH, 128).T for b in (bq, bk, bv)], axis=1))
    bo_pk = np.ascontiguousarray(bo.reshape(NDT, 128).T)

    cst = np.zeros((128, 257), np.float32)
    cst[:, 0:128] = np.eye(128, dtype=np.float32)
    cst[:, 128] = 1.0
    cst[0, 129:257] = 1.0
    pp = np.arange(128)[:, None]
    ff = np.arange(QC)[None, :]
    cstb = np.zeros((128, 641), np.float32)
    cstb[:, 0:128] = np.eye(128, dtype=np.float32)
    cstb[:, 128] = 1.0
    cstb[:, 129:385] = np.where(ff >= pp, 0.0, -60000.0)
    cstb[:, 385:641] = np.where(ff >= 128 + pp, 0.0, -60000.0)
    r2 = np.zeros((4, 64), np.float32)
    for e in range(4):
        r2[e, e * 16:(e + 1) * 16] = 1.0

    shared = dict(wqkv=wqkv, wo=wo, art=art, bt=bt, bqkv=bqkv,
                  bo=bo_pk, cst=cst, cstb=cstb, r2=r2)
    return xT, shared


def emit_stats(nc, gst, blk, XB, bufs, ctrs, gate, pfx):
    sq, S1, S2, BC = bufs["sq"], bufs["S1"], bufs["S2"], bufs["BC"]
    rows, srow, brow = bufs["rows"], bufs["srow"], bufs["brow"]
    sb, mb = bufs["sb"], bufs["mb"]
    cstf, cstb = bufs["cstf"], bufs["cstb"]
    c_sq = _getc(ctrs, gst, nc, pfx + "sq")
    c_sq2 = _getc(ctrs, gst, nc, pfx + "sq2")
    c_st = _getc(ctrs, gst, nc, pfx + "st")
    c_se = _getc(ctrs, gst, nc, pfx + "se")
    c_vr = _getc(ctrs, gst, nc, pfx + "vr")
    c_sr = _getc(ctrs, gst, nc, pfx + "sr")
    c_bc = _getc(ctrs, gst, nc, pfx + "bc")
    c_bcc = _getc(ctrs, gst, nc, pfx + "bcc")
    sq0, st0, se0, vr0, sr0, bc0, bcc0 = (c_sq.n, c_st.n, c_se.n, c_vr.n,
                                          c_sr.n, c_bc.n, c_bcc.n)

    sq20 = c_sq2.n

    @blk.scalar
    def _(s):
        k = 0
        for dt in range(NDT):
            gate(s, dt)
            for c in range(4):
                if k % 2 == 1:
                    if k >= 2:
                        s.wait_ge(c_st.sem, st0 + k - 1)
                    xs = XB[:, dt * T + c * 512: dt * T + c * 512 + 512]
                    c_sq2.inc(s.square(sq[1], xs))
                k += 1
        s.wait_ge(c_st.sem, st0 + 128)
        for c in range(4):
            c_se.inc(s.copy(rows[0][0:1, c * 512:(c + 1) * 512], S1[c]))
            c_se.inc(s.copy(rows[1][0:1, c * 512:(c + 1) * 512], S2[c]))
        s.wait_ge(c_vr.sem, vr0 + 5)
        i1 = s.activation(rows[1][0:1, :], rows[1][0:1, :], AF.Ln)
        c_sr.inc(i1)
        s.wait_ge(c_sr.sem, sr0 + 1)
        c_sr.inc(s.activation(srow[0:1, :], rows[1][0:1, :], AF.Exp, scale=-0.5))
        for i in range(8):
            s.wait_ge(c_bc.sem, bc0 + i + 1)
            dst = sb if i % 2 == 0 else mb
            c = i // 2
            c_bcc.inc(s.copy(dst[:, c * 512:(c + 1) * 512], BC[i % 2][:]))

    @blk.tensor
    def _(t):
        onesb = cstb[:, 128:129]
        k = 0
        for dt in range(NDT):
            gate(t, dt)
            for c in range(4):
                cols = slice(dt * T + c * 512, dt * T + c * 512 + 512)
                t.matmul(S1[c], onesb, XB[:, cols],
                         start=(dt == 0), stop=(dt == NDT - 1),
                         skip_group_check=True)
                if k % 2 == 0:
                    t.wait_ge(c_sq.sem, sq0 + k // 2 + 1)
                else:
                    t.wait_ge(c_sq2.sem, sq20 + k // 2 + 1)
                ins = t.matmul(S2[c], onesb, sq[k % 2],
                               start=(dt == 0), stop=(dt == NDT - 1),
                               skip_group_check=True)
                c_st.inc(ins)
                k += 1
        onesr = cstf[0:1, 129:257]
        for i in range(8):
            if i % 2 == 0:
                t.wait_ge(c_sr.sem, sr0 + 2)
            else:
                t.wait_ge(c_vr.sem, vr0 + 7)
            if i >= 2:
                t.wait_ge(c_bcc.sem, bcc0 + i - 1)
            src = srow if i % 2 == 0 else brow
            c = i // 2
            c_bc.inc(t.matmul(BC[i % 2][:], onesr,
                              src[0:1, c * 512:(c + 1) * 512],
                              start=True, stop=True))

    @blk.vector
    def _(v):
        k = 0
        for dt in range(NDT):
            gate(v, dt)
            for c in range(4):
                if k % 2 == 0:
                    if k >= 2:
                        v.wait_ge(c_st.sem, st0 + k - 1)
                    xs = XB[:, dt * T + c * 512: dt * T + c * 512 + 512]
                    c_sq.inc(v.tensor_tensor(sq[0], xs, xs, ALU.mult))
                k += 1
        v.wait_ge(c_se.sem, se0 + 8)
        _chain(v, c_vr, v.tensor_scalar(rows[0][0:1, :], rows[0][0:1, :],
                                        1.0 / D, None, ALU.mult))
        _chain(v, c_vr, v.tensor_scalar(rows[1][0:1, :], rows[1][0:1, :],
                                        1.0 / D, None, ALU.mult))
        _chain(v, c_vr, v.tensor_tensor(brow[0:1, :], rows[0][0:1, :],
                                        rows[0][0:1, :], ALU.mult))
        _chain(v, c_vr, v.tensor_tensor(rows[1][0:1, :], rows[1][0:1, :],
                                        brow[0:1, :], ALU.subtract))
        _chain(v, c_vr, v.tensor_scalar(rows[1][0:1, :], rows[1][0:1, :],
                                        EPS, None, ALU.add))
        v.wait_ge(c_sr.sem, sr0 + 2)
        _chain(v, c_vr, v.tensor_tensor(brow[0:1, :], rows[0][0:1, :],
                                        srow[0:1, :], ALU.mult))
        _chain(v, c_vr, v.tensor_scalar(brow[0:1, :], brow[0:1, :],
                                        -1.0, None, ALU.mult))


def emit_ffn(nc, gst, blk, XB, bufs, ctrs, gate, pfx, out_dram, c_ost):
    sb, mb = bufs["sb"], bufs["mb"]
    ln2, tmp = bufs["ln2"], bufs["tmp"]
    ML, LG, LGT = bufs["ML"], bufs["LG"], bufs["LGT"]
    GT, GPS, DPS = bufs["GT"], bufs["GPS"], bufs["DPS"]
    midT, midTg = bufs["midT"], bufs["midTg"]
    logT, gT = bufs["logT"], bufs["gT"]
    lmb, indb, ebf, pbf = bufs["lmb"], bufs["indb"], bufs["ebf"], bufs["pbf"]
    m1b, m2b, dsb, rcb, gbf = (bufs["m1b"], bufs["m2b"], bufs["dsb"],
                               bufs["rcb"], bufs["gbf"])
    fout = bufs.get("fout")
    ART, BT, cstf, r2f = bufs["ART"], bufs["BT"], bufs["cstf"], bufs["r2f"]

    names = ["ln", "mlp", "mde", "lgt", "gA", "ge", "gB", "gt", "gte", "g",
             "mtg", "d", "dev", "vch"]
    C = {nm: _getc(ctrs, gst, nc, pfx + nm) for nm in names}
    z = {nm: C[nm].n for nm in names}
    c_bcc = ctrs[pfx + "bcc"]
    bcc_tgt = c_bcc.n
    ost0 = [c.n for c in c_ost] if c_ost is not None else [0, 0]

    @blk.vector
    def _(v):
        v.wait_ge(c_bcc.sem, bcc_tgt)
        for dt in range(NDT):
            gate(v, dt)
            _chain(v, C["vch"], v.tensor_tensor(
                tmp, XB[:, dt * T:(dt + 1) * T], sb, ALU.mult))
            if dt >= 2:
                v.wait_ge(C["mlp"].sem, z["mlp"] + dt - 1)
            ins = v.tensor_tensor(ln2[dt % 2], tmp, mb, ALU.add)
            _chain(v, C["vch"], ins)
            C["ln"].inc(v.nop())
        v.wait_ge(C["lgt"].sem, z["lgt"] + 16)
        for tt in range(16):
            s4 = slice(4 * tt, 4 * tt + 4)
            t1 = slice(tt, tt + 1)
            _chain(v, C["vch"], v.tensor_reduce(
                m1b[:, t1], LGT[:, s4], AX.X, ALU.max))
            _chain(v, C["vch"], v.tensor_scalar(
                lmb[:, s4], LGT[:, s4], m1b[:, t1], None, ALU.subtract))
            _chain(v, C["vch"], v.tensor_scalar(
                indb[:, s4], lmb[:, s4], 0.0, None, ALU.is_equal))
            _chain(v, C["vch"], v.tensor_scalar(
                indb[:, s4], indb[:, s4], NEGBIG, None, ALU.mult))
            _chain(v, C["vch"], v.tensor_tensor(
                pbf[:, s4], lmb[:, s4], indb[:, s4], ALU.add))
            _chain(v, C["vch"], v.tensor_reduce(
                m2b[:, t1], pbf[:, s4], AX.X, ALU.max))
            ins = v.tensor_scalar(indb[:, s4], lmb[:, s4], m2b[:, t1],
                                  None, ALU.is_ge)
            _chain(v, C["vch"], ins)
            C["gA"].inc(v.nop())
        v.wait_ge(C["ge"].sem, z["ge"] + 1)
        for tt in range(16):
            s4 = slice(4 * tt, 4 * tt + 4)
            t1 = slice(tt, tt + 1)
            _chain(v, C["vch"], v.tensor_tensor(
                pbf[:, s4], ebf[:, s4], indb[:, s4], ALU.mult))
            _chain(v, C["vch"], v.tensor_reduce(
                dsb[:, t1], pbf[:, s4], AX.X, ALU.add))
            _chain(v, C["vch"], v.tensor_scalar(
                dsb[:, t1], dsb[:, t1], 0.5, None, ALU.mult))
            _chain(v, C["vch"], v.reciprocal(rcb[:, t1], dsb[:, t1]))
            ins = v.tensor_scalar(gbf[:, s4], pbf[:, s4], rcb[:, t1],
                                  None, ALU.mult)
            _chain(v, C["vch"], ins)
            C["gB"].inc(v.nop())
        for c in range(4):
            v.wait_ge(C["g"].sem, z["g"] + c + 1)
            ins = v.tensor_tensor(midTg[0:64, c * 512:(c + 1) * 512],
                                  midT[0:64, c * 512:(c + 1) * 512],
                                  GPS[c % 2][0:64, :], ALU.mult)
            _chain(v, C["vch"], ins)
            C["mtg"].inc(v.nop())
        k = 0
        for c2 in (0, 2):
            for ot in range(NDT):
                v.wait_ge(C["d"].sem, z["d"] + k + 1)
                colr = slice(ot * T + c2 * 512, ot * T + c2 * 512 + 1024)
                if out_dram is None:
                    ins = v.tensor_tensor(XB[:, colr], XB[:, colr],
                                          DPS[k % 2], ALU.add)
                else:
                    if k >= 2:
                        v.wait_ge(c_ost[k % 2].sem,
                                  ost0[k % 2] + 16 * (k // 2))
                    ins = v.tensor_tensor(fout[k % 2], XB[:, colr],
                                          DPS[k % 2], ALU.add)
                C["dev"].inc(ins)
                k += 1

    @blk.tensor
    def _(t):
        for dt in range(NDT):
            t.wait_ge(C["ln"].sem, z["ln"] + dt + 1)
            for c in range(4):
                cols = slice(c * 512, (c + 1) * 512)
                t.matmul(ML[c][0:64, :], ART[:, dt * 68: dt * 68 + 64],
                         ln2[dt % 2][:, cols],
                         start=(dt == 0), stop=(dt == NDT - 1),
                         skip_group_check=True)
                ins = t.matmul(LG[c][0:4, :],
                               ART[:, dt * 68 + 64: dt * 68 + 68],
                               ln2[dt % 2][:, cols],
                               start=(dt == 0), stop=(dt == NDT - 1),
                               skip_group_check=True)
                if c == 3:
                    C["mlp"].inc(ins)
        t.wait_ge(C["mde"].sem, z["mde"] + 8)
        for tt in range(16):
            ins = t.transpose(LGT[:, 4 * tt: 4 * tt + 4],
                              logT[0:4, tt * 128:(tt + 1) * 128],
                              cstf[0:4, 0:4])
            C["lgt"].inc(ins)
        for c in range(4):
            for j in range(4):
                tt = 4 * c + j
                t.wait_ge(C["gB"].sem, z["gB"] + tt + 1)
                if c >= 1 and j == 0:
                    t.wait_ge(C["gte"].sem, z["gte"] + c)
                ins = t.transpose(GT[0][0:4, j * 128:(j + 1) * 128],
                                  gbf[:, 4 * tt:4 * tt + 4],
                                  cstf[:, 0:128])
                C["gt"].inc(ins)
            t.wait_ge(C["gte"].sem, z["gte"] + c + 1)
            if c >= 2:
                t.wait_ge(C["mtg"].sem, z["mtg"] + c - 1)
            C["g"].inc(t.matmul(GPS[c % 2][0:64, :], r2f[0:4, :],
                                gT[0:4, c * 512:(c + 1) * 512],
                                start=True, stop=True))
        k = 0
        for c2 in (0, 2):
            t.wait_ge(C["mtg"].sem, z["mtg"] + c2 + 2)
            for ot in range(NDT):
                if k >= 2:
                    t.wait_ge(C["dev"].sem, z["dev"] + k - 1)
                t.matmul(DPS[k % 2][:, 0:512],
                         BT[0:64, ot * 128:(ot + 1) * 128],
                         midTg[0:64, c2 * 512:(c2 + 1) * 512],
                         start=True, stop=True, skip_group_check=True)
                ins = t.matmul(DPS[k % 2][:, 512:1024],
                               BT[0:64, ot * 128:(ot + 1) * 128],
                               midTg[0:64, (c2 + 1) * 512:(c2 + 2) * 512],
                               start=True, stop=True, skip_group_check=True)
                C["d"].inc(ins)
                k += 1

    @blk.scalar
    def _(s):
        s.wait_ge(C["mlp"].sem, z["mlp"] + NDT)
        for c in range(4):
            C["mde"].inc(s.copy(midT[0:64, c * 512:(c + 1) * 512],
                                ML[c][0:64, :]))
        for c in range(4):
            C["mde"].inc(s.copy(logT[0:4, c * 512:(c + 1) * 512],
                                LG[c][0:4, :]))
        s.wait_ge(C["gA"].sem, z["gA"] + 16)
        C["ge"].inc(s.activation(ebf, lmb, AF.Exp))
        for c in range(4):
            s.wait_ge(C["gt"].sem, z["gt"] + 4 * (c + 1))
            C["gte"].inc(s.copy(gT[0:4, c * 512:(c + 1) * 512],
                                GT[0][0:4, :]))

    if out_dram is not None:
        @blk.sync
        def _(sy):
            k = 0
            for c2 in (0, 2):
                for ot in range(NDT):
                    sy.wait_ge(C["dev"].sem, z["dev"] + k + 1)
                    ins = sy.dma_start(
                        out=out_dram[ot * 128:(ot + 1) * 128,
                                     c2 * 512:c2 * 512 + 1024],
                        in_=fout[k % 2])
                    c_ost[k % 2].inc(ins, 16)
                    k += 1
            sy.wait_ge(c_ost[0].sem, ost0[0] + 16 * 32)
            sy.wait_ge(c_ost[1].sem, ost0[1] + 16 * 32)


def mk_ffn_bufs(st, nc, with_fout, shared_sb, tag):
    b = dict(shared_sb)
    b["sb"] = st.enter_context(nc.sbuf_tensor(f"{tag}sbb", [128, T], BF16))[:]
    b["mb"] = st.enter_context(nc.sbuf_tensor(f"{tag}mbb", [128, T], BF16))[:]
    b["ART"] = st.enter_context(
        nc.sbuf_tensor(f"{tag}ARTb", [128, NDT * 68], BF16))[:]
    b["BT"] = st.enter_context(nc.sbuf_tensor(f"{tag}BTb", [64, D], BF16))[:]
    if with_fout:
        b["fout"] = [st.enter_context(
            nc.sbuf_tensor(f"{tag}fout{i}", [128, 1024], F32))[:] for i in range(2)]
    with ExitStack() as es:
        b["sq"] = [es.enter_context(
            nc.sbuf_tensor(f"{tag}sq{i}", [128, 512], BF16))[:] for i in range(2)]
        b["rows"] = [es.enter_context(
            nc.sbuf_tensor(f"{tag}row{i}", [1, T], F32))[:] for i in range(2)]
        b["srow"] = es.enter_context(nc.sbuf_tensor(f"{tag}srow", [1, T], F32))[:]
        b["brow"] = es.enter_context(nc.sbuf_tensor(f"{tag}brow", [1, T], F32))[:]
    b["ln2"] = [st.enter_context(nc.sbuf_tensor(f"{tag}ln2_{i}", [128, T], BF16))[:]
                for i in range(2)]
    b["tmp"] = st.enter_context(nc.sbuf_tensor(f"{tag}tmpb", [128, T], BF16))[:]
    b["midT"] = st.enter_context(nc.sbuf_tensor(f"{tag}midT", [64, T], BF16))[:]
    b["midTg"] = st.enter_context(nc.sbuf_tensor(f"{tag}midTg", [64, T], BF16))[:]
    b["logT"] = st.enter_context(nc.sbuf_tensor(f"{tag}logT", [4, T], F32))[:]
    b["gT"] = st.enter_context(nc.sbuf_tensor(f"{tag}gTb", [4, T], F32))[:]
    for nm, w in (("lmb", 64), ("indb", 64), ("ebf", 64), ("pbf", 64),
                  ("m1b", 16), ("m2b", 16), ("dsb", 16), ("rcb", 16),
                  ("gbf", 64)):
        b[nm] = st.enter_context(nc.sbuf_tensor(tag + nm, [128, w], F32))[:]
    with ExitStack() as es:
        b["S1"] = [es.enter_context(nc.psum_tensor(f"{tag}S1_{c}", [1, 512], F32))[:]
                   for c in range(4)]
        b["S2"] = [es.enter_context(nc.psum_tensor(f"{tag}S2_{c}", [1, 512], F32))[:]
                   for c in range(4)]
    with ExitStack() as es:
        b["BC"] = [es.enter_context(
            nc.psum_tensor(f"{tag}BCx{i}", [128, 512], F32))[:] for i in range(2)]
    with ExitStack() as es:
        b["ML"] = [es.enter_context(
            nc.psum_tensor(f"{tag}ML{c}", [68, 512], F32))[:] for c in range(4)]
        b["LG"] = [es.enter_context(
            nc.psum_tensor(f"{tag}LG{c}", [4, 512], F32))[:] for c in range(4)]
    with ExitStack() as es:
        b["LGT"] = es.enter_context(nc.psum_tensor(f"{tag}LGT", [128, 64], F32))[:]
        b["GT"] = [es.enter_context(
            nc.psum_tensor(f"{tag}GT0", [4, 512], F32))[:] for i in range(1)]
        b["GPS"] = [es.enter_context(
            nc.psum_tensor(f"{tag}GPS{i}", [64, 512], F32))[:] for i in range(2)]
        b["DPS"] = [es.enter_context(
            nc.psum_tensor(f"{tag}DPS{i}", [128, 1024], F32))[:] for i in range(2)]
    return b


def build_nc(stop=None):
    nc = bass.Bass("TRN2", target_bir_lowering=False, debug=False)
    xT = nc.dram_tensor("xT", [D, T], F32, kind="ExternalInput").ap()
    wqkv = nc.dram_tensor("wqkv", [NH, 3, 128, D], F32, kind="ExternalInput").ap()
    wo = nc.dram_tensor("wo", [NDT, 128, D], F32, kind="ExternalInput").ap()
    art = nc.dram_tensor("art", [128, NDT * 68], F32, kind="ExternalInput").ap()
    bt = nc.dram_tensor("bt", [64, D], F32, kind="ExternalInput").ap()
    bqkv = nc.dram_tensor("bqkv", [128, 96], F32, kind="ExternalInput").ap()
    bo = nc.dram_tensor("bo", [128, NDT], F32, kind="ExternalInput").ap()
    cst = nc.dram_tensor("cst", [128, 257], F32, kind="ExternalInput").ap()
    cstb_d = nc.dram_tensor("cstb", [128, 641], F32, kind="ExternalInput").ap()
    r2 = nc.dram_tensor("r2", [4, 64], F32, kind="ExternalInput").ap()
    out = nc.dram_tensor("out", [D, T], F32, kind="ExternalOutput").ap()

    xspill = nc.dram_tensor("xspill", [NDT, 128, T], BF16, kind="Internal").ap()
    otspill = nc.dram_tensor("otspill", [NH, 128, T], BF16, kind="Internal").ap()
    x2spill = nc.dram_tensor("x2spill", [NDT, 128, T], BF16, kind="Internal").ap()

    gst = ExitStack()
    with gst:
        XB = gst.enter_context(nc.sbuf_tensor("XB", [128, NDT * T], BF16))[:]
        cstf = gst.enter_context(nc.sbuf_tensor("cstf", [128, 257], F32))[:]
        cstb = gst.enter_context(nc.sbuf_tensor("cstbb", [128, 641], BF16))[:]
        bq_t = gst.enter_context(nc.sbuf_tensor("bqkvb", [128, 96], F32))[:]
        bo_t = gst.enter_context(nc.sbuf_tensor("bob", [128, NDT], F32))[:]
        r2_t = gst.enter_context(nc.sbuf_tensor("r2b", [4, 64], F32))[:]
        shared_sb = dict(cstf=cstf, cstb=cstb, r2f=r2_t)

        ctrs = {}
        c_cst = Ctr(gst, nc, "cst")
        c_xg = [Ctr(gst, nc, f"xg{g}") for g in range(4)]
        c_ost = [Ctr(gst, nc, "ost0"), Ctr(gst, nc, "ost1")]

        with ExitStack() as st, nc.Block() as blk:
            bufs = mk_ffn_bufs(st, nc, False, shared_sb, "z0")

            @blk.gpsimd
            def _(g):
                c_cst.inc(g.dma_start(out=cstb, in_=cstb_d), 16)
                c_cst.inc(g.dma_start(out=cstf, in_=cst), 16)
                c_cst.inc(g.dma_start(out=bufs["ART"], in_=art), 16)
                c_cst.inc(g.dma_start(out=bufs["BT"], in_=bt), 16)
                c_cst.inc(g.dma_start(out=bq_t, in_=bqkv), 16)
                c_cst.inc(g.dma_start(out=bo_t, in_=bo), 16)
                c_cst.inc(g.dma_start(out=r2_t, in_=r2), 16)
                for gi in range(4):
                    for dt in range(8 * gi, 8 * gi + 8):
                        c_xg[gi].inc(g.dma_start(
                            out=XB[:, dt * T:(dt + 1) * T],
                            in_=xT[dt * 128:(dt + 1) * 128, :]), 16)

            def gate0(eng, dt):
                if dt == 0:
                    eng.wait_ge(c_cst.sem, 112)
                if dt % 8 == 0:
                    eng.wait_ge(c_xg[dt // 8].sem, 128)

            emit_stats(nc, gst, blk, XB, bufs, ctrs, gate0, "f")
            emit_ffn(nc, gst, blk, XB, bufs, ctrs, gate0, "f", None, c_ost)

        if stop == "p0":
            _emit_dump_xb(nc, XB, out, c_ost)
            return nc

        with ExitStack() as st, nc.Block() as blk:
            bufs = mk_ffn_bufs(st, nc, False, shared_sb, "z1")
            c_sp = [Ctr(gst, nc, f"sp1{g}") for g in range(4)]
            c_av1 = Ctr(gst, nc, "av1")

            def gate1(eng, dt):
                pass

            emit_stats(nc, gst, blk, XB, bufs, ctrs, gate1, "a")
            sbb, mbb, tmpb = bufs["sb"], bufs["mb"], bufs["tmp"]
            c_bcc = ctrs["abcc"]
            bcc_tgt = c_bcc.n

            @blk.sync
            def _(sy):
                for gi in range(4):
                    for dt in range(8 * gi, 8 * gi + 8):
                        c_sp[gi].inc(sy.dma_start(
                            out=xspill[dt],
                            in_=XB[:, dt * T:(dt + 1) * T]), 16)

            @blk.vector
            def _(v):
                c_vch = Ctr(gst, nc, "av1v")
                v.wait_ge(c_bcc.sem, bcc_tgt)
                for dt in range(NDT):
                    _chain(v, c_vch, v.tensor_tensor(
                        tmpb, XB[:, dt * T:(dt + 1) * T], sbb, ALU.mult))
                    if dt % 8 == 0:
                        v.wait_ge(c_sp[dt // 8].sem, 128)
                    ins = v.tensor_tensor(XB[:, dt * T:(dt + 1) * T],
                                          tmpb, mbb, ALU.add)
                    _chain(v, c_vch, ins)
                    c_av1.inc(v.nop())

        if stop == "p1":
            _emit_dump_xb(nc, XB, out, c_ost)
            return nc

        HH = _DEV_NH or NH
        with ExitStack() as st, nc.Block() as blk:
            W = [st.enter_context(nc.sbuf_tensor(
                f"W{i}", [128, NDT * 128], BF16))[:] for i in range(3)]
            c_w = [Ctr(gst, nc, f"w2_{i}") for i in range(3)]
            Qb = st.enter_context(nc.sbuf_tensor("Qb", [128, T], BF16))[:]
            Kb = st.enter_context(nc.sbuf_tensor("Kb", [128, T], BF16))[:]
            Vb = st.enter_context(nc.sbuf_tensor("Vb", [128, T], BF16))[:]
            Vtok = [st.enter_context(nc.sbuf_tensor(
                f"Vtok{i}", [128, T], BF16))[:] for i in range(2)]
            expT = [st.enter_context(nc.sbuf_tensor(
                f"expT{i}", [128, w * QC], BF16))[:]
                    for i, w in ((0, 14), (1, 16), (2, 12))]
            OTh = st.enter_context(nc.sbuf_tensor("OTh", [128, T], BF16))[:]
            rrow = [st.enter_context(nc.sbuf_tensor(
                f"rrow{i}", [1, QC], F32))[:] for i in range(2)]
            dnb = [st.enter_context(nc.sbuf_tensor(
                f"dnb{i}", [128, QC], BF16))[:] for i in range(2)]
            otu = [st.enter_context(nc.sbuf_tensor(
                f"otu{i}", [128, QC], BF16))[:] for i in range(2)]
            PJ = [st.enter_context(nc.psum_tensor(
                f"PJ{i}", [128, 512], F32))[:] for i in range(2)]
            SC = [st.enter_context(nc.psum_tensor(
                f"SC{i}", [128, 512], F32))[:] for i in range(3)]
            OTP = [st.enter_context(nc.psum_tensor(
                f"OTP{i}", [128, 512], F32))[:] for i in range(2)]
            DNB = st.enter_context(nc.psum_tensor("DNB", [128, QC], F32))[:]

            c_pj = Ctr(gst, nc, "pj2")
            c_qkv = Ctr(gst, nc, "qkv2")
            c_qkvv = Ctr(gst, nc, "qkvv2")
            c_vt = [Ctr(gst, nc, f"vt2_{i}") for i in range(4)]
            c_scg = Ctr(gst, nc, "scg2")
            c_exp = Ctr(gst, nc, "exp2")
            c_otu = Ctr(gst, nc, "otu2")
            c_av = Ctr(gst, nc, "av2")
            c_den = Ctr(gst, nc, "den2")
            c_dmm = Ctr(gst, nc, "dmm2")
            c_rec = Ctr(gst, nc, "rec2")
            c_bcd = Ctr(gst, nc, "bcd2")
            c_dcp = Ctr(gst, nc, "dcp2")
            c_otv = Ctr(gst, nc, "otv2")
            c_osp = Ctr(gst, nc, "osp2")
            c_vv = Ctr(gst, nc, "vv2")

            def ngrp(qc):
                return qc + 1

            def gbase(h, qc):
                return GPH * h + sum(ngrp(u) for u in range(qc))

            @blk.gpsimd
            def _(g):
                for h in range(HH):
                    for idx, m in enumerate((2, 0, 1)):
                        p = 3 * h + idx
                        if p >= 3:
                            g.wait_ge(c_pj.sem, 4 * (p - 2))
                        c_w[p % 3].inc(g.dma_start(out=W[p % 3],
                                                   in_=wqkv[h, m]), 16)

            @blk.tensor
            def _(t):
                onesf = cstf[:, 128:129]
                onesr = cstf[0:1, 129:257]
                idb = cstb[:, 0:128]
                mkn = [cstb[:, 129:385], cstb[:, 385:641]]

                def emit_scores(h, qc):
                    t.wait_ge(c_qkvv.sem, 8 * h + (qc // 2) + 1)
                    t.wait_ge(c_qkvv.sem, 8 * h + 4 + (2 * qc + 1) // 4 + 1)
                    for gi in range(ngrp(qc)):
                        G = gbase(h, qc) + gi
                        if G >= 3:
                            t.wait_ge(c_exp.sem, G - 2)
                        diag = (gi == qc)
                        for j in range(2):
                            kt = 2 * gi + j
                            ins = t.matmul(
                                SC[G % 3][:, j * QC:(j + 1) * QC],
                                Kb[:, kt * 128:(kt + 1) * 128],
                                Qb[:, qc * QC:(qc + 1) * QC],
                                start=True, stop=not diag,
                                skip_group_check=True)
                            if diag:
                                ins = t.matmul(
                                    SC[G % 3][:, j * QC:(j + 1) * QC],
                                    idb, mkn[j],
                                    start=False, stop=(j == 1),
                                    skip_group_check=True)
                        c_scg.inc(ins)

                def emit_av(h, qc):
                    iq = NQC * h + qc
                    t.wait_ge(c_exp.sem, gbase(h, qc) + ngrp(qc))
                    for b in range((2 * qc + 1) // 4 + 1):
                        t.wait_ge(c_vt[b].sem, 64 * (h + 1))
                    if iq >= 2:
                        t.wait_ge(c_otu.sem, iq - 1)
                        t.wait_ge(c_rec.sem, iq - 1)
                    nkt = 2 * qc + 2
                    onesb = cstb[:, 128:129]
                    for kt in range(nkt):
                        t.matmul(OTP[qc % 2][:, 0:QC],
                                 Vtok[h % 2][:, kt * 128:(kt + 1) * 128],
                                 expT[qc % 3][:, kt * QC:(kt + 1) * QC],
                                 start=(kt == 0), stop=False,
                                 skip_group_check=True)
                    for kt in range(nkt):
                        ins = t.matmul(OTP[qc % 2][0:1, QC:2 * QC], onesb,
                                       expT[qc % 3][:, kt * QC:(kt + 1) * QC],
                                       start=False, stop=(kt == nkt - 1),
                                       skip_group_check=True)
                    c_av.inc(ins)

                def emit_dmm(h, qc):
                    pass

                def emit_bcast(h, qc):
                    iq = NQC * h + qc
                    t.wait_ge(c_rec.sem, iq + 1)
                    if iq >= 1:
                        t.wait_ge(c_dcp.sem, iq)
                    ins = t.matmul(DNB, onesr, rrow[qc % 2][0:1, :],
                                   start=True, stop=True)
                    c_bcd.inc(ins)

                def qkv_counts(n):
                    full, rem = divmod(n, 12)
                    return (8 * full + max(0, rem - 4),
                            4 * full + min(rem, 4))

                def emit_projm(h, idx):
                    p = 3 * h + idx
                    t.wait_ge(c_w[p % 3].sem, 16 * (p // 3 + 1))
                    for c in range(4):
                        k12 = 12 * h + 4 * idx + c
                        if k12 >= 2:
                            nv, ns = qkv_counts(k12 - 1)
                            iprev = ((k12 - 2) % 12) // 4
                            if iprev == 0:
                                t.wait_ge(c_qkv.sem, ns)
                            else:
                                t.wait_ge(c_qkvv.sem, nv)
                        for dt in range(NDT):
                            ins = t.matmul(
                                PJ[k12 % 2],
                                W[p % 3][:, dt * 128:(dt + 1) * 128],
                                XB[:, dt * T + c * 512:
                                   dt * T + c * 512 + 512],
                                start=(dt == 0), stop=(dt == NDT - 1))
                        c_pj.inc(ins)

                for h in range(HH):
                    emit_projm(h, 0)
                    if h > 0:
                        emit_av(h - 1, NQC - 2)
                        emit_dmm(h - 1, NQC - 2)
                        emit_bcast(h - 1, NQC - 3)
                    emit_projm(h, 1)
                    if h > 0:
                        emit_av(h - 1, NQC - 1)
                        emit_bcast(h - 1, NQC - 2)
                    emit_projm(h, 2)
                    if h > 0:
                        emit_dmm(h - 1, NQC - 1)
                    for qc in range(NQC):
                        emit_scores(h, qc)
                        if qc == 0 and h > 0:
                            emit_bcast(h - 1, NQC - 1)
                        if qc >= 2:
                            emit_av(h, qc - 2)
                            emit_dmm(h, qc - 2)
                        if qc >= 3:
                            emit_bcast(h, qc - 3)
                emit_av(HH - 1, NQC - 2)
                emit_dmm(HH - 1, NQC - 2)
                emit_bcast(HH - 1, NQC - 3)
                emit_av(HH - 1, NQC - 1)
                emit_bcast(HH - 1, NQC - 2)
                emit_dmm(HH - 1, NQC - 1)
                emit_bcast(HH - 1, NQC - 1)

            @blk.scalar
            def _(s):
                def emit_dcp(iq):
                    s.wait_ge(c_bcd.sem, iq + 1)
                    if iq >= 1:
                        s.wait_ge(c_otv.sem, iq - 1)
                    c_dcp.inc(s.copy(dnb[iq % 2], DNB))

                for h in range(HH):
                    for idx in range(3):
                        if idx == 0:
                            for c in range(4):
                                k12 = 12 * h + c
                                s.wait_ge(c_pj.sem, k12 + 1)
                                if c == 0 and h > 0:
                                    for b in range(4):
                                        s.wait_ge(c_vt[b].sem, 64 * h)
                                ins = s.activation(
                                    Vb[:, c * 512:(c + 1) * 512],
                                    PJ[k12 % 2], AF.Identity,
                                    bias=bq_t[:, 64 + h: 64 + h + 1])
                                c_qkv.inc(ins)
                        if h > 0:
                            emit_dcp(NQC * (h - 1) + NQC - 3 + idx)
                    for qc in range(NQC):
                        iq = NQC * h + qc
                        if iq >= 3:
                            s.wait_ge(c_av.sem, iq - 2)
                        for gi in range(ngrp(qc)):
                            G = gbase(h, qc) + gi
                            s.wait_ge(c_scg.sem, G + 1)
                            ins = s.activation(
                                expT[qc % 3][:, 2 * gi * QC:
                                             2 * gi * QC + 2 * QC],
                                SC[G % 3], AF.Exp, scale=ISQ)
                            c_exp.inc(ins)
                        if qc >= 3:
                            emit_dcp(iq - 3)
                for qc in (NQC - 3, NQC - 2, NQC - 1):
                    emit_dcp(NQC * (HH - 1) + qc)


            @blk.vector
            def _(v):
                def emit_norm(iq):
                    v.wait_ge(c_dcp.sem, iq + 1)
                    if iq % NQC == 0 and iq >= NQC:
                        v.wait_ge(c_osp.sem, 16 * (iq // NQC))
                    qc = iq % NQC
                    ins = v.tensor_tensor(OTh[:, qc * QC:(qc + 1) * QC],
                                          otu[iq % 2], dnb[iq % 2], ALU.mult)
                    _chain(v, c_vv, ins)
                    c_otv.inc(v.nop())

                def emit_copy(iq):
                    v.wait_ge(c_av.sem, iq + 1)
                    ins = v.tensor_copy(otu[iq % 2], OTP[iq % 2][:, 0:QC])
                    _chain(v, c_vv, ins)
                    c_otu.inc(v.nop())

                def emit_recip(iq):
                    v.wait_ge(c_av.sem, iq + 1)
                    ins = v.reciprocal(rrow[iq % 2][0:1, :],
                                       OTP[iq % 2][0:1, QC:2 * QC])
                    _chain(v, c_vv, ins)
                    c_rec.inc(v.nop())

                def vtail(h):
                    base = NQC * h
                    emit_norm(base + NQC - 3)
                    emit_copy(base + NQC - 2)
                    emit_recip(base + NQC - 2)
                    emit_norm(base + NQC - 2)
                    emit_copy(base + NQC - 1)
                    emit_recip(base + NQC - 1)
                    emit_norm(base + NQC - 1)

                def emit_qk_evac(h, m):
                    for c in range(4):
                        k12 = 12 * h + 4 + 4 * m + c
                        v.wait_ge(c_pj.sem, k12 + 1)
                        dst = (Qb, Kb)[m]
                        ins = v.tensor_scalar(
                            dst[:, c * 512:(c + 1) * 512], PJ[k12 % 2],
                            bq_t[:, m * 32 + h: m * 32 + h + 1], None,
                            ALU.add)
                        c_qkvv.inc(ins)

                for h in range(HH):
                    emit_qk_evac(h, 0)
                    if h > 0:
                        base = NQC * (h - 1)
                        emit_norm(base + NQC - 3)
                        emit_copy(base + NQC - 2)
                        emit_recip(base + NQC - 2)
                        emit_norm(base + NQC - 2)
                    emit_qk_evac(h, 1)
                    if h > 0:
                        base = NQC * (h - 1)
                        emit_copy(base + NQC - 1)
                        emit_recip(base + NQC - 1)
                        emit_norm(base + NQC - 1)
                    for j in range(NQC):
                        iq = NQC * h + j
                        if j >= 3:
                            emit_norm(iq - 3)
                        if j >= 2:
                            emit_copy(iq - 2)
                            emit_recip(iq - 2)
                vtail(HH - 1)


            @blk.sync
            def _(sy):
                for h in range(HH):
                    if h >= 2:
                        sy.wait_ge(c_av.sem, NQC * (h - 1))
                    for c in range(4):
                        sy.wait_ge(c_qkv.sem, 4 * h + c + 1)
                        for j in range(4):
                            kt = 4 * c + j
                            ins = sy.dma_start_transpose(
                                Vtok[h % 2][:, kt * 128:(kt + 1) * 128],
                                Vb[:, kt * 128:(kt + 1) * 128])
                            c_vt[c].inc(ins, 16)
                    sy.wait_ge(c_otv.sem, NQC * (h + 1))
                    c_osp.inc(sy.dma_start(out=otspill[h], in_=OTh), 16)
                sy.wait_ge(c_osp.sem, 16 * HH)

        if stop == "p3a":
            with ExitStack() as st, nc.Block() as blk:
                c_otl = Ctr(gst, nc, "otl")

                @blk.gpsimd
                def _(g):
                    hh = _DEV_NH or NH
                    for h in range(hh):
                        c_otl.inc(g.dma_start(out=XB[:, h * T:(h + 1) * T],
                                              in_=otspill[h]), 16)
                    g.wait_ge(c_otl.sem, 16 * hh)
            _emit_dump_xb(nc, XB, out, c_ost)
            return nc

        with ExitStack() as st, nc.Block() as blk:
            slot = [st.enter_context(nc.sbuf_tensor(
                f"wos{i}", [128, NDT * 128], BF16))[:] for i in range(2)]
            x1r = [st.enter_context(nc.sbuf_tensor(
                f"x1r{i}", [128, T], BF16))[:] for i in range(2)]
            x2a = [st.enter_context(nc.sbuf_tensor(
                f"x2a{i}", [128, 512], BF16))[:] for i in range(4)]
            x2b = [st.enter_context(nc.sbuf_tensor(
                f"x2b{i}", [128, 512], BF16))[:] for i in range(4)]
            APs = [st.enter_context(nc.psum_tensor(
                f"APp{i}", [128, 512], F32))[:] for i in range(2)]
            c_otl = Ctr(gst, nc, "otl")
            c_wol = [Ctr(gst, nc, f"wol{i}") for i in range(2)]
            c_x1l = [Ctr(gst, nc, f"x1l{i}") for i in range(2)]
            c_pj3 = Ctr(gst, nc, "pj3")
            c_x2a = Ctr(gst, nc, "x2a")
            c_x2b = Ctr(gst, nc, "x2b")
            c_x2s = [Ctr(gst, nc, f"x2s{i}") for i in range(2)]

            @blk.gpsimd
            def _(g):
                for h in range(NH):
                    c_otl.inc(g.dma_start(out=XB[:, h * T:(h + 1) * T],
                                          in_=otspill[h]), 16)
                for ot in range(NDT):
                    if ot >= 2:
                        g.wait_ge(c_pj3.sem, 4 * (ot - 1))
                    c_wol[ot % 2].inc(g.dma_start(out=slot[ot % 2],
                                                  in_=wo[ot]), 16)
                    if ot >= 2:
                        g.wait_ge(c_x2b.sem, 4 * (ot - 1))
                    c_x1l[ot % 2].inc(g.dma_start(out=x1r[ot % 2],
                                                  in_=xspill[ot]), 16)

            @blk.tensor
            def _(t):
                t.wait_ge(c_otl.sem, 16 * NH)
                for ot in range(NDT):
                    t.wait_ge(c_wol[ot % 2].sem, 16 * (ot // 2 + 1))
                    for c in range(4):
                        k = 4 * ot + c
                        if k >= 2:
                            t.wait_ge(c_x2a.sem, k - 1)
                        for dt in range(NDT):
                            ins = t.matmul(
                                APs[k % 2],
                                slot[ot % 2][:, dt * 128:(dt + 1) * 128],
                                XB[:, dt * T + c * 512:
                                   dt * T + c * 512 + 512],
                                start=(dt == 0), stop=(dt == NDT - 1))
                        c_pj3.inc(ins)

            @blk.scalar
            def _(s):
                for ot in range(NDT):
                    for c in range(4):
                        k = 4 * ot + c
                        s.wait_ge(c_pj3.sem, k + 1)
                        if k >= 4:
                            s.wait_ge(c_x2b.sem, k - 3)
                        ins = s.activation(x2a[k % 4], APs[k % 2],
                                           AF.Identity,
                                           bias=bo_t[:, ot:ot + 1])
                        c_x2a.inc(ins)

            @blk.vector
            def _(v):
                for ot in range(NDT):
                    v.wait_ge(c_x1l[ot % 2].sem, 16 * (ot // 2 + 1))
                    for c in range(4):
                        k = 4 * ot + c
                        v.wait_ge(c_x2a.sem, k + 1)
                        if k >= 4:
                            v.wait_ge(c_x2s[k % 2].sem, 16 * (k // 2 - 1))
                        ins = v.tensor_tensor(
                            x2b[k % 4], x2a[k % 4],
                            x1r[ot % 2][:, c * 512:(c + 1) * 512],
                            ALU.add)
                        c_x2b.inc(ins)

            @blk.sync
            def _(sy):
                for ot in range(NDT):
                    for c in range(4):
                        k = 4 * ot + c
                        sy.wait_ge(c_x2b.sem, k + 1)
                        ins = sy.dma_start(
                            out=x2spill[ot][:, c * 512:(c + 1) * 512],
                            in_=x2b[k % 4])
                        c_x2s[k % 2].inc(ins, 16)
                sy.wait_ge(c_x2s[0].sem, 16 * 64)
                sy.wait_ge(c_x2s[1].sem, 16 * 64)

        if stop == "p3":
            with ExitStack() as st, nc.Block() as blk:
                c_rl = Ctr(gst, nc, "rl3")

                @blk.gpsimd
                def _(g):
                    for dt in range(NDT):
                        c_rl.inc(g.dma_start(out=XB[:, dt * T:(dt + 1) * T],
                                             in_=x2spill[dt]), 16)
                    g.wait_ge(c_rl.sem, 16 * NDT)
            _emit_dump_xb(nc, XB, out, c_ost)
            return nc

        with ExitStack() as st, nc.Block() as blk:
            bufs = mk_ffn_bufs(st, nc, True, shared_sb, "z4")
            c_ab4 = Ctr(gst, nc, "ab4")
            c_x2g = [Ctr(gst, nc, f"x2g{g}") for g in range(4)]

            @blk.gpsimd
            def _(g):
                c_ab4.inc(g.dma_start(out=bufs["ART"], in_=art), 16)
                c_ab4.inc(g.dma_start(out=bufs["BT"], in_=bt), 16)
                for gi in range(4):
                    for dt in range(8 * gi, 8 * gi + 8):
                        c_x2g[gi].inc(g.dma_start(
                            out=XB[:, dt * T:(dt + 1) * T],
                            in_=x2spill[dt]), 16)

            def gate4(eng, dt):
                if dt == 0:
                    eng.wait_ge(c_ab4.sem, 32)
                if dt % 8 == 0:
                    eng.wait_ge(c_x2g[dt // 8].sem, 128)

            emit_stats(nc, gst, blk, XB, bufs, ctrs, gate4, "f")
            emit_ffn(nc, gst, blk, XB, bufs, ctrs, gate4, "f", out, c_ost)

    return nc


def _emit_dump_xb(nc, XB, out, c_ost):
    with ExitStack() as st, nc.Block() as blk:
        fo = [st.enter_context(nc.sbuf_tensor(
            f"dump{i}", [128, T], F32))[:] for i in range(2)]
        c_dv = Ctr(st, nc, "dumpv")
        ost0 = [c.n for c in c_ost]

        @blk.vector
        def _(v):
            for dt in range(NDT):
                if dt >= 2:
                    v.wait_ge(c_ost[dt % 2].sem,
                              ost0[dt % 2] + 16 * (dt // 2))
                ins = v.tensor_copy(fo[dt % 2], XB[:, dt * T:(dt + 1) * T])
                c_dv.inc(ins)

        @blk.sync
        def _(sy):
            for dt in range(NDT):
                sy.wait_ge(c_dv.sem, dt + 1)
                c_ost[dt % 2].inc(
                    sy.dma_start(out=out[dt * 128:(dt + 1) * 128, :],
                                 in_=fo[dt % 2]), 16)
            sy.wait_ge(c_ost[0].sem, ost0[0] + 16 * 16)
            sy.wait_ge(c_ost[1].sem, ost0[1] + 16 * 16)


def kernel(**inputs):
    xT, shared = host_pack(inputs)
    nc = build_nc(stop=_STOP_PHASE)
    in_maps = []
    for core in range(8):
        m = {"xT": np.ascontiguousarray(xT[core])}
        m.update(shared)
        in_maps.append(m)
    res = run_bass_kernel_spmd(nc, in_maps, core_ids=list(range(8)))
    outs = [np.asarray(res.results[i]["out"], np.float32) for i in range(8)]
    return np.ascontiguousarray(np.stack(outs, 0).transpose(0, 2, 1))


# version 17
# speedup vs baseline: 1.0718x; 1.0718x over previous
import numpy as np
import concourse.bass as bass
import concourse.mybir as mybir
from concourse.bass_utils import run_bass_kernel_spmd
from contextlib import ExitStack

F32 = mybir.dt.float32
BF16 = mybir.dt.bfloat16
AF = mybir.ActivationFunctionType
ALU = mybir.AluOpType
AX = mybir.AxisListType

D = 4096
T = 2048
NDT = 32
NH = 32
QC = 256
NQC = T // QC
GPH = 36
EPS = 1e-5
ISQ = 0.08838834764831845
NEGBIG = -1.0e30

_STOP_PHASE = None
_DEV_NH = None


class Ctr:
    def __init__(self, st, nc, name):
        self.sem = st.enter_context(nc.semaphore(name))
        self.n = 0

    def inc(self, ins, k=1):
        ins.then_inc(self.sem, k)
        self.n += k
        return self.n


def _chain(eng, ctr, ins):
    ctr.inc(ins)
    eng.wait_ge(ctr.sem, ctr.n)


def _getc(ctrs, gst, nc, name):
    if name not in ctrs:
        ctrs[name] = Ctr(gst, nc, name)
    return ctrs[name]


def host_pack(inputs):
    x = np.asarray(inputs["L0_attn_x"], np.float32)
    Wq = np.asarray(inputs["L0_attn_Wq"], np.float32)
    Wk = np.asarray(inputs["L0_attn_Wk"], np.float32)
    Wv = np.asarray(inputs["L0_attn_Wv"], np.float32)
    Wo = np.asarray(inputs["L0_attn_Wo"], np.float32)
    bq = np.asarray(inputs["L0_attn_bq"], np.float32)
    bk = np.asarray(inputs["L0_attn_bk"], np.float32)
    bv = np.asarray(inputs["L0_attn_bv"], np.float32)
    bo = np.asarray(inputs["L0_attn_bo"], np.float32)
    rw = np.asarray(inputs["L0_ffn_router_w"], np.float32)
    A = np.asarray(inputs["L0_ffn_A"], np.float32)
    B = np.asarray(inputs["L0_ffn_B"], np.float32)

    xT = np.ascontiguousarray(x.transpose(0, 2, 1))

    def packw(W):
        return W.reshape(NH, 128, NDT, 128).transpose(0, 3, 2, 1)

    wqkv = np.ascontiguousarray(
        np.stack([packw(Wq), packw(Wk), packw(Wv)], axis=1)
    ).reshape(NH, 3, 128, NDT * 128)
    wo = np.ascontiguousarray(packw(Wo)).reshape(NDT, 128, NDT * 128)

    art = np.concatenate([A.reshape(64, D).T, rw.T], axis=1)
    art = np.ascontiguousarray(
        art.reshape(NDT, 128, 68).transpose(1, 0, 2)).reshape(128, NDT * 68)
    bt = np.ascontiguousarray(B.transpose(0, 2, 1)).reshape(64, D)

    bqkv = np.ascontiguousarray(np.concatenate(
        [b.reshape(NH, 128).T for b in (bq, bk, bv)], axis=1))
    bo_pk = np.ascontiguousarray(bo.reshape(NDT, 128).T)

    cst = np.zeros((128, 129), np.float32)
    cst[:, 0] = 1.0
    cst[0, 1:129] = 1.0
    pp = np.arange(128)[:, None]
    ff = np.arange(QC)[None, :]
    cstb = np.zeros((128, 641), np.float32)
    cstb[:, 0:128] = np.eye(128, dtype=np.float32)
    cstb[:, 128] = 1.0
    cstb[:, 129:385] = np.where(ff >= pp, 0.0, -60000.0)
    cstb[:, 385:641] = np.where(ff >= 128 + pp, 0.0, -60000.0)
    r2 = np.zeros((4, 64), np.float32)
    for e in range(4):
        r2[e, e * 16:(e + 1) * 16] = 1.0

    shared = dict(wqkv=wqkv, wo=wo, art=art, bt=bt, bqkv=bqkv,
                  bo=bo_pk, cst=cst, cstb=cstb, r2=r2)
    return xT, shared


def emit_stats(nc, gst, blk, XB, bufs, ctrs, gate, pfx):
    sq, S1, S2, BC = bufs["sq"], bufs["S1"], bufs["S2"], bufs["BC"]
    rows, srow, brow = bufs["rows"], bufs["srow"], bufs["brow"]
    sb, mb = bufs["sb"], bufs["mb"]
    cstf, cstb = bufs["cstf"], bufs["cstb"]
    c_sq = _getc(ctrs, gst, nc, pfx + "sq")
    c_sq2 = _getc(ctrs, gst, nc, pfx + "sq2")
    c_st = _getc(ctrs, gst, nc, pfx + "st")
    c_se = _getc(ctrs, gst, nc, pfx + "se")
    c_vr = _getc(ctrs, gst, nc, pfx + "vr")
    c_sr = _getc(ctrs, gst, nc, pfx + "sr")
    c_bc = _getc(ctrs, gst, nc, pfx + "bc")
    c_bcc = _getc(ctrs, gst, nc, pfx + "bcc")
    sq0, st0, se0, vr0, sr0, bc0, bcc0 = (c_sq.n, c_st.n, c_se.n, c_vr.n,
                                          c_sr.n, c_bc.n, c_bcc.n)

    sq20 = c_sq2.n

    @blk.scalar
    def _(s):
        k = 0
        for dt in range(NDT):
            gate(s, dt)
            for c in range(4):
                if k % 2 == 1:
                    if k >= 2:
                        s.wait_ge(c_st.sem, st0 + k - 1)
                    xs = XB[:, dt * T + c * 512: dt * T + c * 512 + 512]
                    c_sq2.inc(s.square(sq[1], xs))
                k += 1
        s.wait_ge(c_st.sem, st0 + 128)
        for c in range(4):
            c_se.inc(s.copy(rows[0][0:1, c * 512:(c + 1) * 512], S1[c]))
            c_se.inc(s.copy(rows[1][0:1, c * 512:(c + 1) * 512], S2[c]))
        s.wait_ge(c_vr.sem, vr0 + 5)
        i1 = s.activation(rows[1][0:1, :], rows[1][0:1, :], AF.Ln)
        c_sr.inc(i1)
        s.wait_ge(c_sr.sem, sr0 + 1)
        c_sr.inc(s.activation(srow[0:1, :], rows[1][0:1, :], AF.Exp, scale=-0.5))
        for i in range(8):
            s.wait_ge(c_bc.sem, bc0 + i + 1)
            dst = sb if i % 2 == 0 else mb
            c = i // 2
            c_bcc.inc(s.copy(dst[:, c * 512:(c + 1) * 512], BC[i % 2][:]))

    @blk.tensor
    def _(t):
        onesb = cstb[:, 128:129]
        k = 0
        for dt in range(NDT):
            gate(t, dt)
            for c in range(4):
                cols = slice(dt * T + c * 512, dt * T + c * 512 + 512)
                t.matmul(S1[c], onesb, XB[:, cols],
                         start=(dt == 0), stop=(dt == NDT - 1),
                         skip_group_check=True)
                if k % 2 == 0:
                    t.wait_ge(c_sq.sem, sq0 + k // 2 + 1)
                else:
                    t.wait_ge(c_sq2.sem, sq20 + k // 2 + 1)
                ins = t.matmul(S2[c], onesb, sq[k % 2],
                               start=(dt == 0), stop=(dt == NDT - 1),
                               skip_group_check=True)
                c_st.inc(ins)
                k += 1
        onesr = cstf[0:1, 1:129]
        for i in range(8):
            if i % 2 == 0:
                t.wait_ge(c_sr.sem, sr0 + 2)
            else:
                t.wait_ge(c_vr.sem, vr0 + 7)
            if i >= 2:
                t.wait_ge(c_bcc.sem, bcc0 + i - 1)
            src = srow if i % 2 == 0 else brow
            c = i // 2
            c_bc.inc(t.matmul(BC[i % 2][:], onesr,
                              src[0:1, c * 512:(c + 1) * 512],
                              start=True, stop=True))

    @blk.vector
    def _(v):
        k = 0
        for dt in range(NDT):
            gate(v, dt)
            for c in range(4):
                if k % 2 == 0:
                    if k >= 2:
                        v.wait_ge(c_st.sem, st0 + k - 1)
                    xs = XB[:, dt * T + c * 512: dt * T + c * 512 + 512]
                    c_sq.inc(v.tensor_tensor(sq[0], xs, xs, ALU.mult))
                k += 1
        v.wait_ge(c_se.sem, se0 + 8)
        _chain(v, c_vr, v.tensor_scalar(rows[0][0:1, :], rows[0][0:1, :],
                                        1.0 / D, None, ALU.mult))
        _chain(v, c_vr, v.tensor_scalar(rows[1][0:1, :], rows[1][0:1, :],
                                        1.0 / D, None, ALU.mult))
        _chain(v, c_vr, v.tensor_tensor(brow[0:1, :], rows[0][0:1, :],
                                        rows[0][0:1, :], ALU.mult))
        _chain(v, c_vr, v.tensor_tensor(rows[1][0:1, :], rows[1][0:1, :],
                                        brow[0:1, :], ALU.subtract))
        _chain(v, c_vr, v.tensor_scalar(rows[1][0:1, :], rows[1][0:1, :],
                                        EPS, None, ALU.add))
        v.wait_ge(c_sr.sem, sr0 + 2)
        _chain(v, c_vr, v.tensor_tensor(brow[0:1, :], rows[0][0:1, :],
                                        srow[0:1, :], ALU.mult))
        _chain(v, c_vr, v.tensor_scalar(brow[0:1, :], brow[0:1, :],
                                        -1.0, None, ALU.mult))


def emit_ffn(nc, gst, blk, XB, bufs, ctrs, gate, pfx, out_dram, c_ost):
    sb, mb = bufs["sb"], bufs["mb"]
    ln2, tmp = bufs["ln2"], bufs["tmp"]
    ML, LG, LGT = bufs["ML"], bufs["LG"], bufs["LGT"]
    GT, GPS, DPS = bufs["GT"], bufs["GPS"], bufs["DPS"]
    midT, midTg = bufs["midT"], bufs["midTg"]
    logT, gT = bufs["logT"], bufs["gT"]
    lmb, indb, ebf, pbf = bufs["lmb"], bufs["indb"], bufs["ebf"], bufs["pbf"]
    m1b, m2b, dsb, rcb, gbf = (bufs["m1b"], bufs["m2b"], bufs["dsb"],
                               bufs["rcb"], bufs["gbf"])
    fout = bufs.get("fout")
    ART, BT, cstf, r2f = bufs["ART"], bufs["BT"], bufs["cstf"], bufs["r2f"]
    cstb = bufs["cstb"]

    names = ["ln", "mlp", "mde", "lgt", "gA", "ge", "gB", "gt", "gte", "g",
             "mtg", "d", "dev", "vch"]
    C = {nm: _getc(ctrs, gst, nc, pfx + nm) for nm in names}
    z = {nm: C[nm].n for nm in names}
    c_bcc = ctrs[pfx + "bcc"]
    bcc_tgt = c_bcc.n
    ost0 = [c.n for c in c_ost] if c_ost is not None else [0, 0]

    @blk.vector
    def _(v):
        v.wait_ge(c_bcc.sem, bcc_tgt)
        for dt in range(NDT):
            gate(v, dt)
            _chain(v, C["vch"], v.tensor_tensor(
                tmp, XB[:, dt * T:(dt + 1) * T], sb, ALU.mult))
            if dt >= 2:
                v.wait_ge(C["mlp"].sem, z["mlp"] + dt - 1)
            ins = v.tensor_tensor(ln2[dt % 2], tmp, mb, ALU.add)
            _chain(v, C["vch"], ins)
            C["ln"].inc(v.nop())
        v.wait_ge(C["lgt"].sem, z["lgt"] + 16)
        for tt in range(16):
            s4 = slice(4 * tt, 4 * tt + 4)
            t1 = slice(tt, tt + 1)
            _chain(v, C["vch"], v.tensor_reduce(
                m1b[:, t1], LGT[:, s4], AX.X, ALU.max))
            _chain(v, C["vch"], v.tensor_scalar(
                lmb[:, s4], LGT[:, s4], m1b[:, t1], None, ALU.subtract))
            _chain(v, C["vch"], v.tensor_scalar(
                indb[:, s4], lmb[:, s4], 0.0, None, ALU.is_equal))
            _chain(v, C["vch"], v.tensor_scalar(
                indb[:, s4], indb[:, s4], NEGBIG, None, ALU.mult))
            _chain(v, C["vch"], v.tensor_tensor(
                pbf[:, s4], lmb[:, s4], indb[:, s4], ALU.add))
            _chain(v, C["vch"], v.tensor_reduce(
                m2b[:, t1], pbf[:, s4], AX.X, ALU.max))
            ins = v.tensor_scalar(indb[:, s4], lmb[:, s4], m2b[:, t1],
                                  None, ALU.is_ge)
            _chain(v, C["vch"], ins)
            C["gA"].inc(v.nop())
        v.wait_ge(C["ge"].sem, z["ge"] + 1)
        for tt in range(16):
            s4 = slice(4 * tt, 4 * tt + 4)
            t1 = slice(tt, tt + 1)
            _chain(v, C["vch"], v.tensor_tensor(
                pbf[:, s4], ebf[:, s4], indb[:, s4], ALU.mult))
            _chain(v, C["vch"], v.tensor_reduce(
                dsb[:, t1], pbf[:, s4], AX.X, ALU.add))
            _chain(v, C["vch"], v.tensor_scalar(
                dsb[:, t1], dsb[:, t1], 0.5, None, ALU.mult))
            _chain(v, C["vch"], v.reciprocal(rcb[:, t1], dsb[:, t1]))
            ins = v.tensor_scalar(gbf[:, s4], pbf[:, s4], rcb[:, t1],
                                  None, ALU.mult)
            _chain(v, C["vch"], ins)
            C["gB"].inc(v.nop())
        for c in range(4):
            v.wait_ge(C["g"].sem, z["g"] + c + 1)
            ins = v.tensor_tensor(midTg[0:64, c * 512:(c + 1) * 512],
                                  midT[0:64, c * 512:(c + 1) * 512],
                                  GPS[c % 2][0:64, :], ALU.mult)
            _chain(v, C["vch"], ins)
            C["mtg"].inc(v.nop())
        k = 0
        for c2 in (0, 2):
            for ot in range(NDT):
                v.wait_ge(C["d"].sem, z["d"] + k + 1)
                colr = slice(ot * T + c2 * 512, ot * T + c2 * 512 + 1024)
                if out_dram is None:
                    ins = v.tensor_tensor(XB[:, colr], XB[:, colr],
                                          DPS[k % 2], ALU.add)
                else:
                    if k >= 2:
                        v.wait_ge(c_ost[k % 2].sem,
                                  ost0[k % 2] + 16 * (k // 2))
                    ins = v.tensor_tensor(fout[k % 2], XB[:, colr],
                                          DPS[k % 2], ALU.add)
                C["dev"].inc(ins)
                k += 1

    @blk.tensor
    def _(t):
        for dt in range(NDT):
            t.wait_ge(C["ln"].sem, z["ln"] + dt + 1)
            for c in range(4):
                cols = slice(c * 512, (c + 1) * 512)
                t.matmul(ML[c][0:64, :], ART[:, dt * 68: dt * 68 + 64],
                         ln2[dt % 2][:, cols],
                         start=(dt == 0), stop=(dt == NDT - 1),
                         skip_group_check=True)
                ins = t.matmul(LG[c][0:4, :],
                               ART[:, dt * 68 + 64: dt * 68 + 68],
                               ln2[dt % 2][:, cols],
                               start=(dt == 0), stop=(dt == NDT - 1),
                               skip_group_check=True)
                if c == 3:
                    C["mlp"].inc(ins)
        t.wait_ge(C["mde"].sem, z["mde"] + 8)
        for tt in range(16):
            ins = t.transpose(LGT[:, 4 * tt: 4 * tt + 4],
                              logT[0:4, tt * 128:(tt + 1) * 128],
                              cstb[0:4, 0:4])
            C["lgt"].inc(ins)
        for c in range(4):
            for j in range(4):
                tt = 4 * c + j
                t.wait_ge(C["gB"].sem, z["gB"] + tt + 1)
                if c >= 1 and j == 0:
                    t.wait_ge(C["gte"].sem, z["gte"] + c)
                ins = t.transpose(GT[0][0:4, j * 128:(j + 1) * 128],
                                  gbf[:, 4 * tt:4 * tt + 4],
                                  cstb[:, 0:128])
                C["gt"].inc(ins)
            t.wait_ge(C["gte"].sem, z["gte"] + c + 1)
            if c >= 2:
                t.wait_ge(C["mtg"].sem, z["mtg"] + c - 1)
            C["g"].inc(t.matmul(GPS[c % 2][0:64, :], r2f[0:4, :],
                                gT[0:4, c * 512:(c + 1) * 512],
                                start=True, stop=True))
        k = 0
        for c2 in (0, 2):
            t.wait_ge(C["mtg"].sem, z["mtg"] + c2 + 2)
            for ot in range(NDT):
                if k >= 2:
                    t.wait_ge(C["dev"].sem, z["dev"] + k - 1)
                t.matmul(DPS[k % 2][:, 0:512],
                         BT[0:64, ot * 128:(ot + 1) * 128],
                         midTg[0:64, c2 * 512:(c2 + 1) * 512],
                         start=True, stop=True, skip_group_check=True)
                ins = t.matmul(DPS[k % 2][:, 512:1024],
                               BT[0:64, ot * 128:(ot + 1) * 128],
                               midTg[0:64, (c2 + 1) * 512:(c2 + 2) * 512],
                               start=True, stop=True, skip_group_check=True)
                C["d"].inc(ins)
                k += 1

    @blk.scalar
    def _(s):
        s.wait_ge(C["mlp"].sem, z["mlp"] + NDT)
        for c in range(4):
            C["mde"].inc(s.copy(midT[0:64, c * 512:(c + 1) * 512],
                                ML[c][0:64, :]))
        for c in range(4):
            C["mde"].inc(s.copy(logT[0:4, c * 512:(c + 1) * 512],
                                LG[c][0:4, :]))
        s.wait_ge(C["gA"].sem, z["gA"] + 16)
        C["ge"].inc(s.activation(ebf, lmb, AF.Exp))
        for c in range(4):
            s.wait_ge(C["gt"].sem, z["gt"] + 4 * (c + 1))
            C["gte"].inc(s.copy(gT[0:4, c * 512:(c + 1) * 512],
                                GT[0][0:4, :]))

    if out_dram is not None:
        @blk.sync
        def _(sy):
            k = 0
            for c2 in (0, 2):
                for ot in range(NDT):
                    sy.wait_ge(C["dev"].sem, z["dev"] + k + 1)
                    ins = sy.dma_start(
                        out=out_dram[ot * 128:(ot + 1) * 128,
                                     c2 * 512:c2 * 512 + 1024],
                        in_=fout[k % 2])
                    c_ost[k % 2].inc(ins, 16)
                    k += 1
            sy.wait_ge(c_ost[0].sem, ost0[0] + 16 * 32)
            sy.wait_ge(c_ost[1].sem, ost0[1] + 16 * 32)


def mk_ffn_bufs(st, nc, with_fout, shared_sb, tag):
    b = dict(shared_sb)
    b["sb"] = st.enter_context(nc.sbuf_tensor(f"{tag}sbb", [128, T], BF16))[:]
    b["mb"] = st.enter_context(nc.sbuf_tensor(f"{tag}mbb", [128, T], BF16))[:]
    b["ART"] = st.enter_context(
        nc.sbuf_tensor(f"{tag}ARTb", [128, NDT * 68], BF16))[:]
    b["BT"] = st.enter_context(nc.sbuf_tensor(f"{tag}BTb", [64, D], BF16))[:]
    if with_fout:
        b["fout"] = [st.enter_context(
            nc.sbuf_tensor(f"{tag}fout{i}", [128, 1024], F32))[:] for i in range(2)]
    with ExitStack() as es:
        b["sq"] = [es.enter_context(
            nc.sbuf_tensor(f"{tag}sq{i}", [128, 512], BF16))[:] for i in range(2)]
        b["rows"] = [es.enter_context(
            nc.sbuf_tensor(f"{tag}row{i}", [1, T], F32))[:] for i in range(2)]
        b["srow"] = es.enter_context(nc.sbuf_tensor(f"{tag}srow", [1, T], F32))[:]
        b["brow"] = es.enter_context(nc.sbuf_tensor(f"{tag}brow", [1, T], F32))[:]
    b["ln2"] = [st.enter_context(nc.sbuf_tensor(f"{tag}ln2_{i}", [128, T], BF16))[:]
                for i in range(2)]
    b["tmp"] = st.enter_context(nc.sbuf_tensor(f"{tag}tmpb", [128, T], BF16))[:]
    b["midT"] = st.enter_context(nc.sbuf_tensor(f"{tag}midT", [64, T], BF16))[:]
    b["midTg"] = st.enter_context(nc.sbuf_tensor(f"{tag}midTg", [64, T], BF16))[:]
    b["logT"] = st.enter_context(nc.sbuf_tensor(f"{tag}logT", [4, T], BF16))[:]
    b["gT"] = st.enter_context(nc.sbuf_tensor(f"{tag}gTb", [4, T], BF16))[:]
    for nm, w in (("lmb", 64), ("indb", 64), ("ebf", 64), ("pbf", 64),
                  ("m1b", 16), ("m2b", 16), ("dsb", 16), ("rcb", 16)):
        b[nm] = st.enter_context(nc.sbuf_tensor(tag + nm, [128, w], F32))[:]
    b["gbf"] = st.enter_context(nc.sbuf_tensor(tag + "gbf", [128, 64], BF16))[:]
    with ExitStack() as es:
        b["S1"] = [es.enter_context(nc.psum_tensor(f"{tag}S1_{c}", [1, 512], F32))[:]
                   for c in range(4)]
        b["S2"] = [es.enter_context(nc.psum_tensor(f"{tag}S2_{c}", [1, 512], F32))[:]
                   for c in range(4)]
    with ExitStack() as es:
        b["BC"] = [es.enter_context(
            nc.psum_tensor(f"{tag}BCx{i}", [128, 512], F32))[:] for i in range(2)]
    with ExitStack() as es:
        b["ML"] = [es.enter_context(
            nc.psum_tensor(f"{tag}ML{c}", [68, 512], F32))[:] for c in range(4)]
        b["LG"] = [es.enter_context(
            nc.psum_tensor(f"{tag}LG{c}", [4, 512], F32))[:] for c in range(4)]
    with ExitStack() as es:
        b["LGT"] = es.enter_context(nc.psum_tensor(f"{tag}LGT", [128, 64], BF16))[:]
        b["GT"] = [es.enter_context(
            nc.psum_tensor(f"{tag}GT0", [4, 512], BF16))[:] for i in range(1)]
        b["GPS"] = [es.enter_context(
            nc.psum_tensor(f"{tag}GPS{i}", [64, 512], F32))[:] for i in range(2)]
        b["DPS"] = [es.enter_context(
            nc.psum_tensor(f"{tag}DPS{i}", [128, 1024], F32))[:] for i in range(2)]
    return b


def build_nc(stop=None):
    nc = bass.Bass("TRN2", target_bir_lowering=False, debug=False)
    xT = nc.dram_tensor("xT", [D, T], F32, kind="ExternalInput").ap()
    wqkv = nc.dram_tensor("wqkv", [NH, 3, 128, D], F32, kind="ExternalInput").ap()
    wo = nc.dram_tensor("wo", [NDT, 128, D], F32, kind="ExternalInput").ap()
    art = nc.dram_tensor("art", [128, NDT * 68], F32, kind="ExternalInput").ap()
    bt = nc.dram_tensor("bt", [64, D], F32, kind="ExternalInput").ap()
    bqkv = nc.dram_tensor("bqkv", [128, 96], F32, kind="ExternalInput").ap()
    bo = nc.dram_tensor("bo", [128, NDT], F32, kind="ExternalInput").ap()
    cst = nc.dram_tensor("cst", [128, 129], F32, kind="ExternalInput").ap()
    cstb_d = nc.dram_tensor("cstb", [128, 641], F32, kind="ExternalInput").ap()
    r2 = nc.dram_tensor("r2", [4, 64], F32, kind="ExternalInput").ap()
    out = nc.dram_tensor("out", [D, T], F32, kind="ExternalOutput").ap()

    xspill = nc.dram_tensor("xspill", [NDT, 128, T], BF16, kind="Internal").ap()
    otspill = nc.dram_tensor("otspill", [NH, 128, T], BF16, kind="Internal").ap()
    x2spill = nc.dram_tensor("x2spill", [NDT, 128, T], BF16, kind="Internal").ap()

    gst = ExitStack()
    with gst:
        XB = gst.enter_context(nc.sbuf_tensor("XB", [128, NDT * T], BF16))[:]
        cstf = gst.enter_context(nc.sbuf_tensor("cstf", [128, 129], F32))[:]
        cstb = gst.enter_context(nc.sbuf_tensor("cstbb", [128, 641], BF16))[:]
        bq_t = gst.enter_context(nc.sbuf_tensor("bqkvb", [128, 96], BF16))[:]
        bo_t = gst.enter_context(nc.sbuf_tensor("bob", [128, NDT], BF16))[:]
        r2_t = gst.enter_context(nc.sbuf_tensor("r2b", [4, 64], BF16))[:]
        shared_sb = dict(cstf=cstf, cstb=cstb, r2f=r2_t)

        ctrs = {}
        c_cst = Ctr(gst, nc, "cst")
        c_xg = [Ctr(gst, nc, f"xg{g}") for g in range(4)]
        c_ost = [Ctr(gst, nc, "ost0"), Ctr(gst, nc, "ost1")]

        with ExitStack() as st, nc.Block() as blk:
            bufs = mk_ffn_bufs(st, nc, False, shared_sb, "z0")

            @blk.gpsimd
            def _(g):
                c_cst.inc(g.dma_start(out=cstb, in_=cstb_d), 16)
                c_cst.inc(g.dma_start(out=cstf, in_=cst), 16)
                c_cst.inc(g.dma_start(out=bufs["ART"], in_=art), 16)
                c_cst.inc(g.dma_start(out=bufs["BT"], in_=bt), 16)
                c_cst.inc(g.dma_start(out=bq_t, in_=bqkv), 16)
                c_cst.inc(g.dma_start(out=bo_t, in_=bo), 16)
                c_cst.inc(g.dma_start(out=r2_t, in_=r2), 16)
                for gi in range(4):
                    for dt in range(8 * gi, 8 * gi + 8):
                        c_xg[gi].inc(g.dma_start(
                            out=XB[:, dt * T:(dt + 1) * T],
                            in_=xT[dt * 128:(dt + 1) * 128, :]), 16)

            def gate0(eng, dt):
                if dt == 0:
                    eng.wait_ge(c_cst.sem, 112)
                if dt % 8 == 0:
                    eng.wait_ge(c_xg[dt // 8].sem, 128)

            emit_stats(nc, gst, blk, XB, bufs, ctrs, gate0, "f")
            emit_ffn(nc, gst, blk, XB, bufs, ctrs, gate0, "f", None, c_ost)

        if stop == "p0":
            _emit_dump_xb(nc, XB, out, c_ost)
            return nc

        with ExitStack() as st, nc.Block() as blk:
            bufs = mk_ffn_bufs(st, nc, False, shared_sb, "z1")
            c_sp = [Ctr(gst, nc, f"sp1{g}") for g in range(4)]
            c_av1 = Ctr(gst, nc, "av1")

            def gate1(eng, dt):
                pass

            emit_stats(nc, gst, blk, XB, bufs, ctrs, gate1, "a")
            sbb, mbb, tmpb = bufs["sb"], bufs["mb"], bufs["tmp"]
            c_bcc = ctrs["abcc"]
            bcc_tgt = c_bcc.n

            @blk.sync
            def _(sy):
                for gi in range(4):
                    for dt in range(8 * gi, 8 * gi + 8):
                        c_sp[gi].inc(sy.dma_start(
                            out=xspill[dt],
                            in_=XB[:, dt * T:(dt + 1) * T]), 16)

            @blk.vector
            def _(v):
                c_vch = Ctr(gst, nc, "av1v")
                v.wait_ge(c_bcc.sem, bcc_tgt)
                for dt in range(NDT):
                    _chain(v, c_vch, v.tensor_tensor(
                        tmpb, XB[:, dt * T:(dt + 1) * T], sbb, ALU.mult))
                    if dt % 8 == 0:
                        v.wait_ge(c_sp[dt // 8].sem, 128)
                    ins = v.tensor_tensor(XB[:, dt * T:(dt + 1) * T],
                                          tmpb, mbb, ALU.add)
                    _chain(v, c_vch, ins)
                    c_av1.inc(v.nop())

        if stop == "p1":
            _emit_dump_xb(nc, XB, out, c_ost)
            return nc

        HH = _DEV_NH or NH
        with ExitStack() as st, nc.Block() as blk:
            W = [st.enter_context(nc.sbuf_tensor(
                f"W{i}", [128, NDT * 128], BF16))[:] for i in range(3)]
            c_w = [Ctr(gst, nc, f"w2_{i}") for i in range(3)]
            Qb = st.enter_context(nc.sbuf_tensor("Qb", [128, T], BF16))[:]
            Kb = st.enter_context(nc.sbuf_tensor("Kb", [128, T], BF16))[:]
            Vb = st.enter_context(nc.sbuf_tensor("Vb", [128, T], BF16))[:]
            Vtok = [st.enter_context(nc.sbuf_tensor(
                f"Vtok{i}", [128, T], BF16))[:] for i in range(2)]
            expT = [st.enter_context(nc.sbuf_tensor(
                f"expT{i}", [128, w * QC], BF16))[:]
                    for i, w in ((0, 10), (1, 12), (2, 14), (3, 16))]
            OTh = st.enter_context(nc.sbuf_tensor("OTh", [128, T], BF16))[:]
            rrt = st.enter_context(nc.sbuf_tensor("rrt", [1, QC], F32))[:]
            rrow = [rrt[0:1, :], rrt[0:1, :]]
            dnb = [st.enter_context(nc.sbuf_tensor(
                f"dnb{i}", [128, QC], BF16))[:] for i in range(2)]
            otu = [st.enter_context(nc.sbuf_tensor(
                f"otu{i}", [128, QC], BF16))[:] for i in range(2)]
            PJ = [st.enter_context(nc.psum_tensor(
                f"PJ{i}", [128, 512], F32))[:] for i in range(2)]
            SC = [st.enter_context(nc.psum_tensor(
                f"SC{i}", [128, 512], F32))[:] for i in range(3)]
            OTP = [st.enter_context(nc.psum_tensor(
                f"OTP{i}", [128, 512], F32))[:] for i in range(2)]
            DNB = st.enter_context(nc.psum_tensor("DNB", [128, QC], F32))[:]

            c_pj = Ctr(gst, nc, "pj2")
            c_qkv = Ctr(gst, nc, "qkv2")
            c_vt = [Ctr(gst, nc, f"vt2_{i}") for i in range(4)]
            c_scg = Ctr(gst, nc, "scg2")
            c_exp = Ctr(gst, nc, "exp2")
            c_otu = Ctr(gst, nc, "otu2")
            c_av = Ctr(gst, nc, "av2")
            c_den = Ctr(gst, nc, "den2")
            c_dmm = Ctr(gst, nc, "dmm2")
            c_rec = Ctr(gst, nc, "rec2")
            c_bcd = Ctr(gst, nc, "bcd2")
            c_dcp = Ctr(gst, nc, "dcp2")
            c_otv = Ctr(gst, nc, "otv2")
            c_osp = Ctr(gst, nc, "osp2")
            c_vv = Ctr(gst, nc, "vv2")

            def ngrp(qc):
                return qc + 1

            def gbase(h, qc):
                return GPH * h + sum(ngrp(u) for u in range(qc))

            @blk.gpsimd
            def _(g):
                for h in range(HH):
                    for m in range(3):
                        p = 3 * h + m
                        if p >= 3:
                            g.wait_ge(c_pj.sem, 4 * (p - 2))
                        c_w[p % 3].inc(g.dma_start(out=W[p % 3],
                                                   in_=wqkv[h, m]), 16)

            @blk.tensor
            def _(t):
                onesf = cstf[:, 0:1]
                onesr = cstf[0:1, 1:129]
                idb = cstb[:, 0:128]
                mkn = [cstb[:, 129:385], cstb[:, 385:641]]

                def emit_scores(h, qc):
                    t.wait_ge(c_qkv.sem, 12 * h + (qc // 2) + 1)
                    t.wait_ge(c_qkv.sem, 12 * h + 4 + (2 * qc + 1) // 4 + 1)
                    for gi in range(ngrp(qc)):
                        G = gbase(h, qc) + gi
                        if G >= 3:
                            t.wait_ge(c_exp.sem, G - 2)
                        diag = (gi == qc)
                        for j in range(2):
                            kt = 2 * gi + j
                            ins = t.matmul(
                                SC[G % 3][:, j * QC:(j + 1) * QC],
                                Kb[:, kt * 128:(kt + 1) * 128],
                                Qb[:, qc * QC:(qc + 1) * QC],
                                start=True, stop=not diag,
                                skip_group_check=True)
                            if diag:
                                ins = t.matmul(
                                    SC[G % 3][:, j * QC:(j + 1) * QC],
                                    idb, mkn[j],
                                    start=False, stop=(j == 1),
                                    skip_group_check=True)
                        c_scg.inc(ins)

                def emit_av(h, qc):
                    iq = NQC * h + qc
                    t.wait_ge(c_exp.sem, gbase(h, qc) + ngrp(qc))
                    for b in range((2 * qc + 1) // 4 + 1):
                        t.wait_ge(c_vt[b].sem, 64 * (h + 1))
                    if iq >= 2:
                        t.wait_ge(c_otu.sem, iq - 1)
                        t.wait_ge(c_rec.sem, iq - 1)
                    nkt = 2 * qc + 2
                    onesb = cstb[:, 128:129]
                    for kt in range(nkt):
                        t.matmul(OTP[qc % 2][:, 0:QC],
                                 Vtok[h % 2][:, kt * 128:(kt + 1) * 128],
                                 expT[qc % 4][:, kt * QC:(kt + 1) * QC],
                                 start=(kt == 0), stop=False,
                                 skip_group_check=True)
                    for kt in range(nkt):
                        ins = t.matmul(OTP[qc % 2][0:1, QC:2 * QC], onesb,
                                       expT[qc % 4][:, kt * QC:(kt + 1) * QC],
                                       start=False, stop=(kt == nkt - 1),
                                       skip_group_check=True)
                    c_av.inc(ins)

                def emit_dmm(h, qc):
                    pass

                def emit_bcast(h, qc):
                    iq = NQC * h + qc
                    t.wait_ge(c_rec.sem, iq + 1)
                    if iq >= 1:
                        t.wait_ge(c_dcp.sem, iq)
                    ins = t.matmul(DNB, onesr, rrow[qc % 2][0:1, :],
                                   start=True, stop=True)
                    c_bcd.inc(ins)

                def emit_projm(h, m):
                    p = 3 * h + m
                    t.wait_ge(c_w[p % 3].sem, 16 * (p // 3 + 1))
                    for c in range(4):
                        k12 = 12 * h + 4 * m + c
                        if k12 >= 2:
                            t.wait_ge(c_qkv.sem, k12 - 1)
                        for dt in range(NDT):
                            ins = t.matmul(
                                PJ[k12 % 2],
                                W[p % 3][:, dt * 128:(dt + 1) * 128],
                                XB[:, dt * T + c * 512:
                                   dt * T + c * 512 + 512],
                                start=(dt == 0), stop=(dt == NDT - 1))
                        c_pj.inc(ins)

                for h in range(HH):
                    emit_projm(h, 0)
                    if h > 0:
                        emit_bcast(h - 1, NQC - 4)
                        emit_av(h - 1, NQC - 3)
                    emit_projm(h, 1)
                    if h > 0:
                        emit_bcast(h - 1, NQC - 3)
                        emit_av(h - 1, NQC - 2)
                    emit_projm(h, 2)
                    if h > 0:
                        emit_bcast(h - 1, NQC - 2)
                        emit_av(h - 1, NQC - 1)
                    for qc in range(NQC):
                        emit_scores(h, qc)
                        if qc == 0 and h > 0:
                            emit_bcast(h - 1, NQC - 1)
                        if qc >= 4:
                            emit_bcast(h, qc - 4)
                        if qc >= 3:
                            emit_av(h, qc - 3)
                emit_bcast(HH - 1, NQC - 4)
                emit_av(HH - 1, NQC - 3)
                emit_bcast(HH - 1, NQC - 3)
                emit_av(HH - 1, NQC - 2)
                emit_bcast(HH - 1, NQC - 2)
                emit_av(HH - 1, NQC - 1)
                emit_bcast(HH - 1, NQC - 1)

            @blk.scalar
            def _(s):
                def emit_dcp(iq):
                    s.wait_ge(c_bcd.sem, iq + 1)
                    if iq >= 1:
                        s.wait_ge(c_otv.sem, iq - 1)
                    c_dcp.inc(s.copy(dnb[iq % 2], DNB))

                for h in range(HH):
                    for m in range(3):
                        for c in range(4):
                            k12 = 12 * h + 4 * m + c
                            s.wait_ge(c_pj.sem, k12 + 1)
                            if m == 2 and c == 0 and h > 0:
                                for b in range(4):
                                    s.wait_ge(c_vt[b].sem, 64 * h)
                            dst = (Qb, Kb, Vb)[m]
                            ins = s.activation(
                                dst[:, c * 512:(c + 1) * 512], PJ[k12 % 2],
                                AF.Identity,
                                bias=bq_t[:, m * 32 + h: m * 32 + h + 1])
                            c_qkv.inc(ins)
                        if h > 0:
                            emit_dcp(NQC * (h - 1) + NQC - 4 + m)
                    if h > 0:
                        emit_dcp(NQC * (h - 1) + NQC - 1)
                    for qc in range(NQC):
                        iq = NQC * h + qc
                        if iq >= 4:
                            s.wait_ge(c_av.sem, iq - 3)
                        for gi in range(ngrp(qc)):
                            G = gbase(h, qc) + gi
                            s.wait_ge(c_scg.sem, G + 1)
                            ins = s.activation(
                                expT[qc % 4][:, 2 * gi * QC:
                                             2 * gi * QC + 2 * QC],
                                SC[G % 3], AF.Exp, scale=ISQ)
                            c_exp.inc(ins)
                        if qc >= 4:
                            emit_dcp(iq - 4)
                for qc in (NQC - 4, NQC - 3, NQC - 2, NQC - 1):
                    emit_dcp(NQC * (HH - 1) + qc)


            @blk.vector
            def _(v):
                def emit_norm(iq):
                    v.wait_ge(c_dcp.sem, iq + 1)
                    if iq % NQC == 0 and iq >= NQC:
                        v.wait_ge(c_osp.sem, 16 * (iq // NQC))
                    qc = iq % NQC
                    ins = v.tensor_tensor(OTh[:, qc * QC:(qc + 1) * QC],
                                          otu[iq % 2], dnb[iq % 2], ALU.mult)
                    _chain(v, c_vv, ins)
                    c_otv.inc(v.nop())

                def emit_copy(iq):
                    v.wait_ge(c_av.sem, iq + 1)
                    ins = v.tensor_copy(otu[iq % 2], OTP[iq % 2][:, 0:QC])
                    _chain(v, c_vv, ins)
                    c_otu.inc(v.nop())

                def emit_recip(iq):
                    v.wait_ge(c_av.sem, iq + 1)
                    ins = v.reciprocal(rrow[iq % 2][0:1, :],
                                       OTP[iq % 2][0:1, QC:2 * QC])
                    _chain(v, c_vv, ins)
                    c_rec.inc(v.nop())

                def vtail(h):
                    base = NQC * h
                    emit_norm(base + NQC - 5)
                    emit_copy(base + NQC - 3)
                    emit_recip(base + NQC - 3)
                    emit_norm(base + NQC - 4)
                    emit_copy(base + NQC - 2)
                    emit_recip(base + NQC - 2)
                    emit_norm(base + NQC - 3)
                    emit_copy(base + NQC - 1)
                    emit_recip(base + NQC - 1)
                    emit_norm(base + NQC - 2)
                    emit_norm(base + NQC - 1)

                for h in range(HH):
                    if h > 0:
                        vtail(h - 1)
                    for j in range(NQC):
                        iq = NQC * h + j
                        if j >= 5:
                            emit_norm(iq - 5)
                        if j >= 3:
                            emit_copy(iq - 3)
                            emit_recip(iq - 3)
                vtail(HH - 1)

            @blk.sync
            def _(sy):
                for h in range(HH):
                    if h >= 2:
                        sy.wait_ge(c_av.sem, NQC * (h - 1))
                    for c in range(4):
                        sy.wait_ge(c_qkv.sem, 12 * h + 8 + c + 1)
                        for j in range(4):
                            kt = 4 * c + j
                            ins = sy.dma_start_transpose(
                                Vtok[h % 2][:, kt * 128:(kt + 1) * 128],
                                Vb[:, kt * 128:(kt + 1) * 128])
                            c_vt[c].inc(ins, 16)
                    sy.wait_ge(c_otv.sem, NQC * (h + 1))
                    c_osp.inc(sy.dma_start(out=otspill[h], in_=OTh), 16)
                sy.wait_ge(c_osp.sem, 16 * HH)

        if stop == "p3a":
            with ExitStack() as st, nc.Block() as blk:
                c_otl = Ctr(gst, nc, "otl")

                @blk.gpsimd
                def _(g):
                    hh = _DEV_NH or NH
                    for h in range(hh):
                        c_otl.inc(g.dma_start(out=XB[:, h * T:(h + 1) * T],
                                              in_=otspill[h]), 16)
                    g.wait_ge(c_otl.sem, 16 * hh)
            _emit_dump_xb(nc, XB, out, c_ost)
            return nc

        with ExitStack() as st, nc.Block() as blk:
            slot = [st.enter_context(nc.sbuf_tensor(
                f"wos{i}", [128, NDT * 128], BF16))[:] for i in range(2)]
            x1r = [st.enter_context(nc.sbuf_tensor(
                f"x1r{i}", [128, T], BF16))[:] for i in range(2)]
            x2a = [st.enter_context(nc.sbuf_tensor(
                f"x2a{i}", [128, 512], BF16))[:] for i in range(4)]
            x2b = [st.enter_context(nc.sbuf_tensor(
                f"x2b{i}", [128, 512], BF16))[:] for i in range(4)]
            APs = [st.enter_context(nc.psum_tensor(
                f"APp{i}", [128, 512], F32))[:] for i in range(2)]
            c_otl = Ctr(gst, nc, "otl")
            c_wol = [Ctr(gst, nc, f"wol{i}") for i in range(2)]
            c_x1l = [Ctr(gst, nc, f"x1l{i}") for i in range(2)]
            c_pj3 = Ctr(gst, nc, "pj3")
            c_x2a = Ctr(gst, nc, "x2a")
            c_x2b = Ctr(gst, nc, "x2b")
            c_x2s = [Ctr(gst, nc, f"x2s{i}") for i in range(2)]

            @blk.gpsimd
            def _(g):
                for h in range(NH):
                    c_otl.inc(g.dma_start(out=XB[:, h * T:(h + 1) * T],
                                          in_=otspill[h]), 16)
                for ot in range(NDT):
                    if ot >= 2:
                        g.wait_ge(c_pj3.sem, 4 * (ot - 1))
                    c_wol[ot % 2].inc(g.dma_start(out=slot[ot % 2],
                                                  in_=wo[ot]), 16)
                    if ot >= 2:
                        g.wait_ge(c_x2b.sem, 4 * (ot - 1))
                    c_x1l[ot % 2].inc(g.dma_start(out=x1r[ot % 2],
                                                  in_=xspill[ot]), 16)

            @blk.tensor
            def _(t):
                t.wait_ge(c_otl.sem, 16 * NH)
                for ot in range(NDT):
                    t.wait_ge(c_wol[ot % 2].sem, 16 * (ot // 2 + 1))
                    for c in range(4):
                        k = 4 * ot + c
                        if k >= 2:
                            t.wait_ge(c_x2a.sem, k - 1)
                        for dt in range(NDT):
                            ins = t.matmul(
                                APs[k % 2],
                                slot[ot % 2][:, dt * 128:(dt + 1) * 128],
                                XB[:, dt * T + c * 512:
                                   dt * T + c * 512 + 512],
                                start=(dt == 0), stop=(dt == NDT - 1))
                        c_pj3.inc(ins)

            @blk.scalar
            def _(s):
                for ot in range(NDT):
                    for c in range(4):
                        k = 4 * ot + c
                        s.wait_ge(c_pj3.sem, k + 1)
                        if k >= 4:
                            s.wait_ge(c_x2b.sem, k - 3)
                        ins = s.activation(x2a[k % 4], APs[k % 2],
                                           AF.Identity,
                                           bias=bo_t[:, ot:ot + 1])
                        c_x2a.inc(ins)

            @blk.vector
            def _(v):
                for ot in range(NDT):
                    v.wait_ge(c_x1l[ot % 2].sem, 16 * (ot // 2 + 1))
                    for c in range(4):
                        k = 4 * ot + c
                        v.wait_ge(c_x2a.sem, k + 1)
                        if k >= 4:
                            v.wait_ge(c_x2s[k % 2].sem, 16 * (k // 2 - 1))
                        ins = v.tensor_tensor(
                            x2b[k % 4], x2a[k % 4],
                            x1r[ot % 2][:, c * 512:(c + 1) * 512],
                            ALU.add)
                        c_x2b.inc(ins)

            @blk.sync
            def _(sy):
                for ot in range(NDT):
                    for c in range(4):
                        k = 4 * ot + c
                        sy.wait_ge(c_x2b.sem, k + 1)
                        ins = sy.dma_start(
                            out=x2spill[ot][:, c * 512:(c + 1) * 512],
                            in_=x2b[k % 4])
                        c_x2s[k % 2].inc(ins, 16)
                sy.wait_ge(c_x2s[0].sem, 16 * 64)
                sy.wait_ge(c_x2s[1].sem, 16 * 64)

        if stop == "p3":
            with ExitStack() as st, nc.Block() as blk:
                c_rl = Ctr(gst, nc, "rl3")

                @blk.gpsimd
                def _(g):
                    for dt in range(NDT):
                        c_rl.inc(g.dma_start(out=XB[:, dt * T:(dt + 1) * T],
                                             in_=x2spill[dt]), 16)
                    g.wait_ge(c_rl.sem, 16 * NDT)
            _emit_dump_xb(nc, XB, out, c_ost)
            return nc

        with ExitStack() as st, nc.Block() as blk:
            bufs = mk_ffn_bufs(st, nc, True, shared_sb, "z4")
            c_ab4 = Ctr(gst, nc, "ab4")
            c_x2g = [Ctr(gst, nc, f"x2g{g}") for g in range(4)]

            @blk.gpsimd
            def _(g):
                c_ab4.inc(g.dma_start(out=bufs["ART"], in_=art), 16)
                c_ab4.inc(g.dma_start(out=bufs["BT"], in_=bt), 16)
                for gi in range(4):
                    for dt in range(8 * gi, 8 * gi + 8):
                        c_x2g[gi].inc(g.dma_start(
                            out=XB[:, dt * T:(dt + 1) * T],
                            in_=x2spill[dt]), 16)

            def gate4(eng, dt):
                if dt == 0:
                    eng.wait_ge(c_ab4.sem, 32)
                if dt % 8 == 0:
                    eng.wait_ge(c_x2g[dt // 8].sem, 128)

            emit_stats(nc, gst, blk, XB, bufs, ctrs, gate4, "f")
            emit_ffn(nc, gst, blk, XB, bufs, ctrs, gate4, "f", out, c_ost)

    return nc


def _emit_dump_xb(nc, XB, out, c_ost):
    with ExitStack() as st, nc.Block() as blk:
        fo = [st.enter_context(nc.sbuf_tensor(
            f"dump{i}", [128, T], F32))[:] for i in range(2)]
        c_dv = Ctr(st, nc, "dumpv")
        ost0 = [c.n for c in c_ost]

        @blk.vector
        def _(v):
            for dt in range(NDT):
                if dt >= 2:
                    v.wait_ge(c_ost[dt % 2].sem,
                              ost0[dt % 2] + 16 * (dt // 2))
                ins = v.tensor_copy(fo[dt % 2], XB[:, dt * T:(dt + 1) * T])
                c_dv.inc(ins)

        @blk.sync
        def _(sy):
            for dt in range(NDT):
                sy.wait_ge(c_dv.sem, dt + 1)
                c_ost[dt % 2].inc(
                    sy.dma_start(out=out[dt * 128:(dt + 1) * 128, :],
                                 in_=fo[dt % 2]), 16)
            sy.wait_ge(c_ost[0].sem, ost0[0] + 16 * 16)
            sy.wait_ge(c_ost[1].sem, ost0[1] + 16 * 16)


def kernel(**inputs):
    xT, shared = host_pack(inputs)
    nc = build_nc(stop=_STOP_PHASE)
    in_maps = []
    for core in range(8):
        m = {"xT": np.ascontiguousarray(xT[core])}
        m.update(shared)
        in_maps.append(m)
    res = run_bass_kernel_spmd(nc, in_maps, core_ids=list(range(8)))
    outs = [np.asarray(res.results[i]["out"], np.float32) for i in range(8)]
    return np.ascontiguousarray(np.stack(outs, 0).transpose(0, 2, 1))


# version 18
# speedup vs baseline: 1.0840x; 1.0114x over previous
import numpy as np
import concourse.bass as bass
import concourse.mybir as mybir
from concourse.bass_utils import run_bass_kernel_spmd
from contextlib import ExitStack

F32 = mybir.dt.float32
BF16 = mybir.dt.bfloat16
AF = mybir.ActivationFunctionType
ALU = mybir.AluOpType
AX = mybir.AxisListType

D = 4096
T = 2048
NDT = 32
NH = 32
QC = 256
NQC = T // QC
GPH = 36
EPS = 1e-5
ISQ = 0.08838834764831845
NEGBIG = -1.0e30

_STOP_PHASE = None
_DEV_NH = None


class Ctr:
    def __init__(self, st, nc, name):
        self.sem = st.enter_context(nc.semaphore(name))
        self.n = 0

    def inc(self, ins, k=1):
        ins.then_inc(self.sem, k)
        self.n += k
        return self.n


def _chain(eng, ctr, ins):
    ctr.inc(ins)
    eng.wait_ge(ctr.sem, ctr.n)


def _getc(ctrs, gst, nc, name):
    if name not in ctrs:
        ctrs[name] = Ctr(gst, nc, name)
    return ctrs[name]


def host_pack(inputs):
    x = np.asarray(inputs["L0_attn_x"], np.float32)
    Wq = np.asarray(inputs["L0_attn_Wq"], np.float32)
    Wk = np.asarray(inputs["L0_attn_Wk"], np.float32)
    Wv = np.asarray(inputs["L0_attn_Wv"], np.float32)
    Wo = np.asarray(inputs["L0_attn_Wo"], np.float32)
    bq = np.asarray(inputs["L0_attn_bq"], np.float32)
    bk = np.asarray(inputs["L0_attn_bk"], np.float32)
    bv = np.asarray(inputs["L0_attn_bv"], np.float32)
    bo = np.asarray(inputs["L0_attn_bo"], np.float32)
    rw = np.asarray(inputs["L0_ffn_router_w"], np.float32)
    A = np.asarray(inputs["L0_ffn_A"], np.float32)
    B = np.asarray(inputs["L0_ffn_B"], np.float32)

    xT = np.ascontiguousarray(x.transpose(0, 2, 1))

    def packw(W):
        return W.reshape(NH, 128, NDT, 128).transpose(0, 3, 2, 1)

    wqkv = np.ascontiguousarray(
        np.stack([packw(Wq), packw(Wk), packw(Wv)], axis=1)
    ).reshape(NH, 3, 128, NDT * 128)
    wo = np.ascontiguousarray(packw(Wo)).reshape(NDT, 128, NDT * 128)

    art = np.concatenate([A.reshape(64, D).T, rw.T], axis=1)
    art = np.ascontiguousarray(
        art.reshape(NDT, 128, 68).transpose(1, 0, 2)).reshape(128, NDT * 68)
    bt = np.ascontiguousarray(B.transpose(0, 2, 1)).reshape(64, D)

    bqkv = np.ascontiguousarray(np.concatenate(
        [b.reshape(NH, 128).T for b in (bq, bk, bv)], axis=1))
    bo_pk = np.ascontiguousarray(bo.reshape(NDT, 128).T)

    cst = np.zeros((128, 129), np.float32)
    cst[:, 0] = 1.0
    cst[0, 1:129] = 1.0
    pp = np.arange(128)[:, None]
    ff = np.arange(QC)[None, :]
    cstb = np.zeros((128, 641), np.float32)
    cstb[:, 0:128] = np.eye(128, dtype=np.float32)
    cstb[:, 128] = 1.0
    cstb[:, 129:385] = np.where(ff >= pp, 0.0, -60000.0)
    cstb[:, 385:641] = np.where(ff >= 128 + pp, 0.0, -60000.0)
    r2 = np.zeros((4, 64), np.float32)
    for e in range(4):
        r2[e, e * 16:(e + 1) * 16] = 1.0

    shared = dict(wqkv=wqkv, wo=wo, art=art, bt=bt, bqkv=bqkv,
                  bo=bo_pk, cst=cst, cstb=cstb, r2=r2)
    return xT, shared


def emit_stats(nc, gst, blk, XB, bufs, ctrs, gate, pfx):
    sq, S1, S2, BC = bufs["sq"], bufs["S1"], bufs["S2"], bufs["BC"]
    rows, srow, brow = bufs["rows"], bufs["srow"], bufs["brow"]
    sb, mb = bufs["sb"], bufs["mb"]
    cstf, cstb = bufs["cstf"], bufs["cstb"]
    c_sq = _getc(ctrs, gst, nc, pfx + "sq")
    c_sq2 = _getc(ctrs, gst, nc, pfx + "sq2")
    c_st = _getc(ctrs, gst, nc, pfx + "st")
    c_se = _getc(ctrs, gst, nc, pfx + "se")
    c_vr = _getc(ctrs, gst, nc, pfx + "vr")
    c_sr = _getc(ctrs, gst, nc, pfx + "sr")
    c_bc = _getc(ctrs, gst, nc, pfx + "bc")
    c_bcc = _getc(ctrs, gst, nc, pfx + "bcc")
    sq0, st0, se0, vr0, sr0, bc0, bcc0 = (c_sq.n, c_st.n, c_se.n, c_vr.n,
                                          c_sr.n, c_bc.n, c_bcc.n)

    sq20 = c_sq2.n

    @blk.scalar
    def _(s):
        k = 0
        for dt in range(NDT):
            gate(s, dt)
            for c in range(4):
                if k % 2 == 1:
                    if k >= 2:
                        s.wait_ge(c_st.sem, st0 + k - 1)
                    xs = XB[:, dt * T + c * 512: dt * T + c * 512 + 512]
                    c_sq2.inc(s.square(sq[1], xs))
                k += 1
        s.wait_ge(c_st.sem, st0 + 128)
        for c in range(4):
            c_se.inc(s.copy(rows[0][0:1, c * 512:(c + 1) * 512], S1[c]))
            c_se.inc(s.copy(rows[1][0:1, c * 512:(c + 1) * 512], S2[c]))
        s.wait_ge(c_vr.sem, vr0 + 5)
        i1 = s.activation(rows[1][0:1, :], rows[1][0:1, :], AF.Ln)
        c_sr.inc(i1)
        s.wait_ge(c_sr.sem, sr0 + 1)
        c_sr.inc(s.activation(srow[0:1, :], rows[1][0:1, :], AF.Exp, scale=-0.5))
        for i in range(8):
            s.wait_ge(c_bc.sem, bc0 + i + 1)
            dst = sb if i % 2 == 0 else mb
            c = i // 2
            c_bcc.inc(s.copy(dst[:, c * 512:(c + 1) * 512], BC[i % 2][:]))

    @blk.tensor
    def _(t):
        onesb = cstb[:, 128:129]
        k = 0
        for dt in range(NDT):
            gate(t, dt)
            for c in range(4):
                cols = slice(dt * T + c * 512, dt * T + c * 512 + 512)
                t.matmul(S1[c], onesb, XB[:, cols],
                         start=(dt == 0), stop=(dt == NDT - 1),
                         skip_group_check=True)
                if k % 2 == 0:
                    t.wait_ge(c_sq.sem, sq0 + k // 2 + 1)
                else:
                    t.wait_ge(c_sq2.sem, sq20 + k // 2 + 1)
                ins = t.matmul(S2[c], onesb, sq[k % 2],
                               start=(dt == 0), stop=(dt == NDT - 1),
                               skip_group_check=True)
                c_st.inc(ins)
                k += 1
        onesr = cstf[0:1, 1:129]
        for i in range(8):
            if i % 2 == 0:
                t.wait_ge(c_sr.sem, sr0 + 2)
            else:
                t.wait_ge(c_vr.sem, vr0 + 7)
            if i >= 2:
                t.wait_ge(c_bcc.sem, bcc0 + i - 1)
            src = srow if i % 2 == 0 else brow
            c = i // 2
            c_bc.inc(t.matmul(BC[i % 2][:], onesr,
                              src[0:1, c * 512:(c + 1) * 512],
                              start=True, stop=True))

    @blk.vector
    def _(v):
        k = 0
        for dt in range(NDT):
            gate(v, dt)
            for c in range(4):
                if k % 2 == 0:
                    if k >= 2:
                        v.wait_ge(c_st.sem, st0 + k - 1)
                    xs = XB[:, dt * T + c * 512: dt * T + c * 512 + 512]
                    c_sq.inc(v.tensor_tensor(sq[0], xs, xs, ALU.mult))
                k += 1
        v.wait_ge(c_se.sem, se0 + 8)
        _chain(v, c_vr, v.tensor_scalar(rows[0][0:1, :], rows[0][0:1, :],
                                        1.0 / D, None, ALU.mult))
        _chain(v, c_vr, v.tensor_scalar(rows[1][0:1, :], rows[1][0:1, :],
                                        1.0 / D, None, ALU.mult))
        _chain(v, c_vr, v.tensor_tensor(brow[0:1, :], rows[0][0:1, :],
                                        rows[0][0:1, :], ALU.mult))
        _chain(v, c_vr, v.tensor_tensor(rows[1][0:1, :], rows[1][0:1, :],
                                        brow[0:1, :], ALU.subtract))
        _chain(v, c_vr, v.tensor_scalar(rows[1][0:1, :], rows[1][0:1, :],
                                        EPS, None, ALU.add))
        v.wait_ge(c_sr.sem, sr0 + 2)
        _chain(v, c_vr, v.tensor_tensor(brow[0:1, :], rows[0][0:1, :],
                                        srow[0:1, :], ALU.mult))
        _chain(v, c_vr, v.tensor_scalar(brow[0:1, :], brow[0:1, :],
                                        -1.0, None, ALU.mult))


def emit_ffn(nc, gst, blk, XB, bufs, ctrs, gate, pfx, out_dram, c_ost):
    sb, mb = bufs["sb"], bufs["mb"]
    ln2, tmp = bufs["ln2"], bufs["tmp"]
    ML, LG, LGT = bufs["ML"], bufs["LG"], bufs["LGT"]
    GT, GPS, DPS = bufs["GT"], bufs["GPS"], bufs["DPS"]
    midT, midTg = bufs["midT"], bufs["midTg"]
    logT, gT = bufs["logT"], bufs["gT"]
    lmb, indb, ebf, pbf = bufs["lmb"], bufs["indb"], bufs["ebf"], bufs["pbf"]
    m1b, m2b, dsb, rcb, gbf = (bufs["m1b"], bufs["m2b"], bufs["dsb"],
                               bufs["rcb"], bufs["gbf"])
    fout = bufs.get("fout")
    ART, BT, cstf, r2f = bufs["ART"], bufs["BT"], bufs["cstf"], bufs["r2f"]
    cstb = bufs["cstb"]

    names = ["ln", "mlp", "mde", "lgt", "gA", "ge", "gB", "gt", "gte", "g",
             "mtg", "d", "dev", "vch"]
    C = {nm: _getc(ctrs, gst, nc, pfx + nm) for nm in names}
    z = {nm: C[nm].n for nm in names}
    c_bcc = ctrs[pfx + "bcc"]
    bcc_tgt = c_bcc.n
    ost0 = [c.n for c in c_ost] if c_ost is not None else [0, 0]

    @blk.vector
    def _(v):
        v.wait_ge(c_bcc.sem, bcc_tgt)
        for dt in range(NDT):
            gate(v, dt)
            _chain(v, C["vch"], v.tensor_tensor(
                tmp, XB[:, dt * T:(dt + 1) * T], sb, ALU.mult))
            if dt >= 2:
                v.wait_ge(C["mlp"].sem, z["mlp"] + dt - 1)
            ins = v.tensor_tensor(ln2[dt % 2], tmp, mb, ALU.add)
            _chain(v, C["vch"], ins)
            C["ln"].inc(v.nop())
        v.wait_ge(C["lgt"].sem, z["lgt"] + 16)
        for tt in range(16):
            s4 = slice(4 * tt, 4 * tt + 4)
            t1 = slice(tt, tt + 1)
            _chain(v, C["vch"], v.tensor_reduce(
                m1b[:, t1], LGT[:, s4], AX.X, ALU.max))
            _chain(v, C["vch"], v.tensor_scalar(
                lmb[:, s4], LGT[:, s4], m1b[:, t1], None, ALU.subtract))
            _chain(v, C["vch"], v.tensor_scalar(
                indb[:, s4], lmb[:, s4], 0.0, None, ALU.is_equal))
            _chain(v, C["vch"], v.tensor_scalar(
                indb[:, s4], indb[:, s4], NEGBIG, None, ALU.mult))
            _chain(v, C["vch"], v.tensor_tensor(
                pbf[:, s4], lmb[:, s4], indb[:, s4], ALU.add))
            _chain(v, C["vch"], v.tensor_reduce(
                m2b[:, t1], pbf[:, s4], AX.X, ALU.max))
            ins = v.tensor_scalar(indb[:, s4], lmb[:, s4], m2b[:, t1],
                                  None, ALU.is_ge)
            _chain(v, C["vch"], ins)
            C["gA"].inc(v.nop())
        v.wait_ge(C["ge"].sem, z["ge"] + 1)
        for tt in range(16):
            s4 = slice(4 * tt, 4 * tt + 4)
            t1 = slice(tt, tt + 1)
            _chain(v, C["vch"], v.tensor_tensor(
                pbf[:, s4], ebf[:, s4], indb[:, s4], ALU.mult))
            _chain(v, C["vch"], v.tensor_reduce(
                dsb[:, t1], pbf[:, s4], AX.X, ALU.add))
            _chain(v, C["vch"], v.tensor_scalar(
                dsb[:, t1], dsb[:, t1], 0.5, None, ALU.mult))
            _chain(v, C["vch"], v.reciprocal(rcb[:, t1], dsb[:, t1]))
            ins = v.tensor_scalar(gbf[:, s4], pbf[:, s4], rcb[:, t1],
                                  None, ALU.mult)
            _chain(v, C["vch"], ins)
            C["gB"].inc(v.nop())
        for c in range(4):
            v.wait_ge(C["g"].sem, z["g"] + c + 1)
            ins = v.tensor_tensor(midTg[0:64, c * 512:(c + 1) * 512],
                                  midT[0:64, c * 512:(c + 1) * 512],
                                  GPS[c % 2][0:64, :], ALU.mult)
            _chain(v, C["vch"], ins)
            C["mtg"].inc(v.nop())
        k = 0
        for c2 in (0, 2):
            for ot in range(NDT):
                v.wait_ge(C["d"].sem, z["d"] + k + 1)
                colr = slice(ot * T + c2 * 512, ot * T + c2 * 512 + 1024)
                if out_dram is None:
                    ins = v.tensor_tensor(XB[:, colr], XB[:, colr],
                                          DPS[k % 2], ALU.add)
                else:
                    if k >= 2:
                        v.wait_ge(c_ost[k % 2].sem,
                                  ost0[k % 2] + 16 * (k // 2))
                    ins = v.tensor_tensor(fout[k % 2], XB[:, colr],
                                          DPS[k % 2], ALU.add)
                C["dev"].inc(ins)
                k += 1

    @blk.tensor
    def _(t):
        for dt in range(NDT):
            t.wait_ge(C["ln"].sem, z["ln"] + dt + 1)
            for c in range(4):
                cols = slice(c * 512, (c + 1) * 512)
                t.matmul(ML[c][0:64, :], ART[:, dt * 68: dt * 68 + 64],
                         ln2[dt % 2][:, cols],
                         start=(dt == 0), stop=(dt == NDT - 1),
                         skip_group_check=True)
                ins = t.matmul(LG[c][0:4, :],
                               ART[:, dt * 68 + 64: dt * 68 + 68],
                               ln2[dt % 2][:, cols],
                               start=(dt == 0), stop=(dt == NDT - 1),
                               skip_group_check=True)
                if c == 3:
                    C["mlp"].inc(ins)
        t.wait_ge(C["mde"].sem, z["mde"] + 8)
        for tt in range(16):
            ins = t.transpose(LGT[:, 4 * tt: 4 * tt + 4],
                              logT[0:4, tt * 128:(tt + 1) * 128],
                              cstb[0:4, 0:4])
            C["lgt"].inc(ins)
        for c in range(4):
            for j in range(4):
                tt = 4 * c + j
                t.wait_ge(C["gB"].sem, z["gB"] + tt + 1)
                if c >= 1 and j == 0:
                    t.wait_ge(C["gte"].sem, z["gte"] + c)
                ins = t.transpose(GT[0][0:4, j * 128:(j + 1) * 128],
                                  gbf[:, 4 * tt:4 * tt + 4],
                                  cstb[:, 0:128])
                C["gt"].inc(ins)
            t.wait_ge(C["gte"].sem, z["gte"] + c + 1)
            if c >= 2:
                t.wait_ge(C["mtg"].sem, z["mtg"] + c - 1)
            C["g"].inc(t.matmul(GPS[c % 2][0:64, :], r2f[0:4, :],
                                gT[0:4, c * 512:(c + 1) * 512],
                                start=True, stop=True))
        k = 0
        for c2 in (0, 2):
            t.wait_ge(C["mtg"].sem, z["mtg"] + c2 + 2)
            for ot in range(NDT):
                if k >= 2:
                    t.wait_ge(C["dev"].sem, z["dev"] + k - 1)
                t.matmul(DPS[k % 2][:, 0:512],
                         BT[0:64, ot * 128:(ot + 1) * 128],
                         midTg[0:64, c2 * 512:(c2 + 1) * 512],
                         start=True, stop=True, skip_group_check=True)
                ins = t.matmul(DPS[k % 2][:, 512:1024],
                               BT[0:64, ot * 128:(ot + 1) * 128],
                               midTg[0:64, (c2 + 1) * 512:(c2 + 2) * 512],
                               start=True, stop=True, skip_group_check=True)
                C["d"].inc(ins)
                k += 1

    @blk.scalar
    def _(s):
        s.wait_ge(C["mlp"].sem, z["mlp"] + NDT)
        for c in range(4):
            C["mde"].inc(s.copy(midT[0:64, c * 512:(c + 1) * 512],
                                ML[c][0:64, :]))
        for c in range(4):
            C["mde"].inc(s.copy(logT[0:4, c * 512:(c + 1) * 512],
                                LG[c][0:4, :]))
        s.wait_ge(C["gA"].sem, z["gA"] + 16)
        C["ge"].inc(s.activation(ebf, lmb, AF.Exp))
        for c in range(4):
            s.wait_ge(C["gt"].sem, z["gt"] + 4 * (c + 1))
            C["gte"].inc(s.copy(gT[0:4, c * 512:(c + 1) * 512],
                                GT[0][0:4, :]))

    if out_dram is not None:
        @blk.sync
        def _(sy):
            k = 0
            for c2 in (0, 2):
                for ot in range(NDT):
                    sy.wait_ge(C["dev"].sem, z["dev"] + k + 1)
                    ins = sy.dma_start(
                        out=out_dram[ot * 128:(ot + 1) * 128,
                                     c2 * 512:c2 * 512 + 1024],
                        in_=fout[k % 2])
                    c_ost[k % 2].inc(ins, 16)
                    k += 1
            sy.wait_ge(c_ost[0].sem, ost0[0] + 16 * 32)
            sy.wait_ge(c_ost[1].sem, ost0[1] + 16 * 32)


def mk_ffn_bufs(st, nc, with_fout, shared_sb, tag):
    b = dict(shared_sb)
    b["sb"] = st.enter_context(nc.sbuf_tensor(f"{tag}sbb", [128, T], BF16))[:]
    b["mb"] = st.enter_context(nc.sbuf_tensor(f"{tag}mbb", [128, T], BF16))[:]
    b["ART"] = st.enter_context(
        nc.sbuf_tensor(f"{tag}ARTb", [128, NDT * 68], BF16))[:]
    b["BT"] = st.enter_context(nc.sbuf_tensor(f"{tag}BTb", [64, D], BF16))[:]
    if with_fout:
        b["fout"] = [st.enter_context(
            nc.sbuf_tensor(f"{tag}fout{i}", [128, 1024], F32))[:] for i in range(2)]
    with ExitStack() as es:
        b["sq"] = [es.enter_context(
            nc.sbuf_tensor(f"{tag}sq{i}", [128, 512], BF16))[:] for i in range(2)]
        b["rows"] = [es.enter_context(
            nc.sbuf_tensor(f"{tag}row{i}", [1, T], F32))[:] for i in range(2)]
        b["srow"] = es.enter_context(nc.sbuf_tensor(f"{tag}srow", [1, T], F32))[:]
        b["brow"] = es.enter_context(nc.sbuf_tensor(f"{tag}brow", [1, T], F32))[:]
    b["ln2"] = [st.enter_context(nc.sbuf_tensor(f"{tag}ln2_{i}", [128, T], BF16))[:]
                for i in range(2)]
    b["tmp"] = st.enter_context(nc.sbuf_tensor(f"{tag}tmpb", [128, T], BF16))[:]
    b["midT"] = st.enter_context(nc.sbuf_tensor(f"{tag}midT", [64, T], BF16))[:]
    b["midTg"] = st.enter_context(nc.sbuf_tensor(f"{tag}midTg", [64, T], BF16))[:]
    b["logT"] = st.enter_context(nc.sbuf_tensor(f"{tag}logT", [4, T], BF16))[:]
    b["gT"] = st.enter_context(nc.sbuf_tensor(f"{tag}gTb", [4, T], BF16))[:]
    for nm, w in (("lmb", 64), ("indb", 64), ("ebf", 64), ("pbf", 64),
                  ("m1b", 16), ("m2b", 16), ("dsb", 16), ("rcb", 16)):
        b[nm] = st.enter_context(nc.sbuf_tensor(tag + nm, [128, w], F32))[:]
    b["gbf"] = st.enter_context(nc.sbuf_tensor(tag + "gbf", [128, 64], BF16))[:]
    with ExitStack() as es:
        b["S1"] = [es.enter_context(nc.psum_tensor(f"{tag}S1_{c}", [1, 512], F32))[:]
                   for c in range(4)]
        b["S2"] = [es.enter_context(nc.psum_tensor(f"{tag}S2_{c}", [1, 512], F32))[:]
                   for c in range(4)]
    with ExitStack() as es:
        b["BC"] = [es.enter_context(
            nc.psum_tensor(f"{tag}BCx{i}", [128, 512], F32))[:] for i in range(2)]
    with ExitStack() as es:
        b["ML"] = [es.enter_context(
            nc.psum_tensor(f"{tag}ML{c}", [68, 512], F32))[:] for c in range(4)]
        b["LG"] = [es.enter_context(
            nc.psum_tensor(f"{tag}LG{c}", [4, 512], F32))[:] for c in range(4)]
    with ExitStack() as es:
        b["LGT"] = es.enter_context(nc.psum_tensor(f"{tag}LGT", [128, 64], BF16))[:]
        b["GT"] = [es.enter_context(
            nc.psum_tensor(f"{tag}GT0", [4, 512], BF16))[:] for i in range(1)]
        b["GPS"] = [es.enter_context(
            nc.psum_tensor(f"{tag}GPS{i}", [64, 512], F32))[:] for i in range(2)]
        b["DPS"] = [es.enter_context(
            nc.psum_tensor(f"{tag}DPS{i}", [128, 1024], F32))[:] for i in range(2)]
    return b


def build_nc(stop=None):
    nc = bass.Bass("TRN2", target_bir_lowering=False, debug=False)
    xT = nc.dram_tensor("xT", [D, T], F32, kind="ExternalInput").ap()
    wqkv = nc.dram_tensor("wqkv", [NH, 3, 128, D], F32, kind="ExternalInput").ap()
    wo = nc.dram_tensor("wo", [NDT, 128, D], F32, kind="ExternalInput").ap()
    art = nc.dram_tensor("art", [128, NDT * 68], F32, kind="ExternalInput").ap()
    bt = nc.dram_tensor("bt", [64, D], F32, kind="ExternalInput").ap()
    bqkv = nc.dram_tensor("bqkv", [128, 96], F32, kind="ExternalInput").ap()
    bo = nc.dram_tensor("bo", [128, NDT], F32, kind="ExternalInput").ap()
    cst = nc.dram_tensor("cst", [128, 129], F32, kind="ExternalInput").ap()
    cstb_d = nc.dram_tensor("cstb", [128, 641], F32, kind="ExternalInput").ap()
    r2 = nc.dram_tensor("r2", [4, 64], F32, kind="ExternalInput").ap()
    out = nc.dram_tensor("out", [D, T], F32, kind="ExternalOutput").ap()

    xspill = nc.dram_tensor("xspill", [NDT, 128, T], BF16, kind="Internal").ap()
    otspill = nc.dram_tensor("otspill", [NH, 128, T], BF16, kind="Internal").ap()
    x2spill = nc.dram_tensor("x2spill", [NDT, 128, T], BF16, kind="Internal").ap()

    gst = ExitStack()
    with gst:
        XB = gst.enter_context(nc.sbuf_tensor("XB", [128, NDT * T], BF16))[:]
        cstf = gst.enter_context(nc.sbuf_tensor("cstf", [128, 129], F32))[:]
        cstb = gst.enter_context(nc.sbuf_tensor("cstbb", [128, 641], BF16))[:]
        bq_t = gst.enter_context(nc.sbuf_tensor("bqkvb", [128, 96], BF16))[:]
        bo_t = gst.enter_context(nc.sbuf_tensor("bob", [128, NDT], BF16))[:]
        r2_t = gst.enter_context(nc.sbuf_tensor("r2b", [4, 64], BF16))[:]
        shared_sb = dict(cstf=cstf, cstb=cstb, r2f=r2_t)

        ctrs = {}
        c_cst = Ctr(gst, nc, "cst")
        c_xg = [Ctr(gst, nc, f"xg{g}") for g in range(4)]
        c_ost = [Ctr(gst, nc, "ost0"), Ctr(gst, nc, "ost1")]

        with ExitStack() as st, nc.Block() as blk:
            bufs = mk_ffn_bufs(st, nc, False, shared_sb, "z0")

            @blk.gpsimd
            def _(g):
                c_cst.inc(g.dma_start(out=cstb, in_=cstb_d), 16)
                c_cst.inc(g.dma_start(out=cstf, in_=cst), 16)
                c_cst.inc(g.dma_start(out=bufs["ART"], in_=art), 16)
                c_cst.inc(g.dma_start(out=bufs["BT"], in_=bt), 16)
                c_cst.inc(g.dma_start(out=bq_t, in_=bqkv), 16)
                c_cst.inc(g.dma_start(out=bo_t, in_=bo), 16)
                c_cst.inc(g.dma_start(out=r2_t, in_=r2), 16)
                for gi in range(4):
                    for dt in range(8 * gi, 8 * gi + 8):
                        c_xg[gi].inc(g.dma_start(
                            out=XB[:, dt * T:(dt + 1) * T],
                            in_=xT[dt * 128:(dt + 1) * 128, :]), 16)

            def gate0(eng, dt):
                if dt == 0:
                    eng.wait_ge(c_cst.sem, 112)
                if dt % 8 == 0:
                    eng.wait_ge(c_xg[dt // 8].sem, 128)

            emit_stats(nc, gst, blk, XB, bufs, ctrs, gate0, "f")
            emit_ffn(nc, gst, blk, XB, bufs, ctrs, gate0, "f", None, c_ost)

        if stop == "p0":
            _emit_dump_xb(nc, XB, out, c_ost)
            return nc

        with ExitStack() as st, nc.Block() as blk:
            bufs = mk_ffn_bufs(st, nc, False, shared_sb, "z1")
            c_sp = [Ctr(gst, nc, f"sp1{g}") for g in range(4)]
            c_av1 = Ctr(gst, nc, "av1")

            def gate1(eng, dt):
                pass

            emit_stats(nc, gst, blk, XB, bufs, ctrs, gate1, "a")
            sbb, mbb, tmpb = bufs["sb"], bufs["mb"], bufs["tmp"]
            c_bcc = ctrs["abcc"]
            bcc_tgt = c_bcc.n

            @blk.sync
            def _(sy):
                for gi in range(4):
                    for dt in range(8 * gi, 8 * gi + 8):
                        c_sp[gi].inc(sy.dma_start(
                            out=xspill[dt],
                            in_=XB[:, dt * T:(dt + 1) * T]), 16)

            @blk.vector
            def _(v):
                c_vch = Ctr(gst, nc, "av1v")
                v.wait_ge(c_bcc.sem, bcc_tgt)
                for dt in range(NDT):
                    _chain(v, c_vch, v.tensor_tensor(
                        tmpb, XB[:, dt * T:(dt + 1) * T], sbb, ALU.mult))
                    if dt % 8 == 0:
                        v.wait_ge(c_sp[dt // 8].sem, 128)
                    ins = v.tensor_tensor(XB[:, dt * T:(dt + 1) * T],
                                          tmpb, mbb, ALU.add)
                    _chain(v, c_vch, ins)
                    c_av1.inc(v.nop())

        if stop == "p1":
            _emit_dump_xb(nc, XB, out, c_ost)
            return nc

        HH = _DEV_NH or NH
        with ExitStack() as st, nc.Block() as blk:
            W = [st.enter_context(nc.sbuf_tensor(
                f"W{i}", [128, NDT * 128], BF16))[:] for i in range(3)]
            c_w = [Ctr(gst, nc, f"w2_{i}") for i in range(3)]
            Qb = st.enter_context(nc.sbuf_tensor("Qb", [128, T], BF16))[:]
            Kb = st.enter_context(nc.sbuf_tensor("Kb", [128, T], BF16))[:]
            Vb = st.enter_context(nc.sbuf_tensor("Vb", [128, T], BF16))[:]
            Vtok = [st.enter_context(nc.sbuf_tensor(
                f"Vtok{i}", [128, T], BF16))[:] for i in range(2)]
            expT = [st.enter_context(nc.sbuf_tensor(
                f"expT{i}", [128, w * QC], BF16))[:]
                    for i, w in ((0, 10), (1, 12), (2, 14), (3, 16))]
            OTh = st.enter_context(nc.sbuf_tensor("OTh", [128, T], BF16))[:]
            rrt = st.enter_context(nc.sbuf_tensor("rrt", [1, QC], F32))[:]
            rrow = [rrt[0:1, :], rrt[0:1, :]]
            dnb = [st.enter_context(nc.sbuf_tensor(
                f"dnb{i}", [128, QC], BF16))[:] for i in range(2)]
            otu = [st.enter_context(nc.sbuf_tensor(
                f"otu{i}", [128, QC], BF16))[:] for i in range(2)]
            PJ = [st.enter_context(nc.psum_tensor(
                f"PJ{i}", [128, 512], F32))[:] for i in range(2)]
            SC = [st.enter_context(nc.psum_tensor(
                f"SC{i}", [128, 512], F32))[:] for i in range(3)]
            OTP = [st.enter_context(nc.psum_tensor(
                f"OTP{i}", [128, 512], F32))[:] for i in range(2)]
            DNB = st.enter_context(nc.psum_tensor("DNB", [128, QC], F32))[:]

            c_pj = Ctr(gst, nc, "pj2")
            c_qkv = Ctr(gst, nc, "qkv2")
            c_vt = [Ctr(gst, nc, f"vt2_{i}") for i in range(4)]
            c_scg = Ctr(gst, nc, "scg2")
            c_exp = Ctr(gst, nc, "exp2")
            c_otu = Ctr(gst, nc, "otu2")
            c_av = Ctr(gst, nc, "av2")
            c_den = Ctr(gst, nc, "den2")
            c_dmm = Ctr(gst, nc, "dmm2")
            c_rec = Ctr(gst, nc, "rec2")
            c_bcd = Ctr(gst, nc, "bcd2")
            c_dcp = Ctr(gst, nc, "dcp2")
            c_otv = Ctr(gst, nc, "otv2")
            c_osp = Ctr(gst, nc, "osp2")
            c_vv = Ctr(gst, nc, "vv2")

            def ngrp(qc):
                return qc + 1

            def gbase(h, qc):
                return GPH * h + sum(ngrp(u) for u in range(qc))

            @blk.gpsimd
            def _(g):
                for h in range(HH):
                    for m in range(3):
                        p = 3 * h + m
                        if p >= 3:
                            g.wait_ge(c_pj.sem, 4 * (p - 2))
                        c_w[p % 3].inc(g.dma_start(out=W[p % 3],
                                                   in_=wqkv[h, m]), 16)

            @blk.tensor
            def _(t):
                onesf = cstf[:, 0:1]
                onesr = cstf[0:1, 1:129]
                idb = cstb[:, 0:128]
                mkn = [cstb[:, 129:385], cstb[:, 385:641]]

                def emit_scores(h, qc):
                    t.wait_ge(c_qkv.sem, 12 * h + (qc // 2) + 1)
                    t.wait_ge(c_qkv.sem, 12 * h + 4 + (2 * qc + 1) // 4 + 1)
                    for gi in range(ngrp(qc)):
                        G = gbase(h, qc) + gi
                        if G >= 3:
                            t.wait_ge(c_exp.sem, G - 2)
                        diag = (gi == qc)
                        for j in range(2):
                            kt = 2 * gi + j
                            ins = t.matmul(
                                SC[G % 3][:, j * QC:(j + 1) * QC],
                                Kb[:, kt * 128:(kt + 1) * 128],
                                Qb[:, qc * QC:(qc + 1) * QC],
                                start=True, stop=not diag,
                                skip_group_check=True)
                            if diag:
                                ins = t.matmul(
                                    SC[G % 3][:, j * QC:(j + 1) * QC],
                                    idb, mkn[j],
                                    start=False, stop=(j == 1),
                                    skip_group_check=True)
                        c_scg.inc(ins)

                def emit_av(h, qc):
                    iq = NQC * h + qc
                    t.wait_ge(c_exp.sem, gbase(h, qc) + ngrp(qc))
                    for b in range((2 * qc + 1) // 4 + 1):
                        t.wait_ge(c_vt[b].sem, 64 * (h + 1))
                    if iq >= 2:
                        t.wait_ge(c_otu.sem, iq - 1)
                        t.wait_ge(c_rec.sem, iq - 1)
                    nkt = 2 * qc + 2
                    onesb = cstb[:, 128:129]
                    for kt in range(nkt):
                        t.matmul(OTP[qc % 2][:, 0:QC],
                                 Vtok[h % 2][:, kt * 128:(kt + 1) * 128],
                                 expT[qc % 4][:, kt * QC:(kt + 1) * QC],
                                 start=(kt == 0), stop=False,
                                 skip_group_check=True)
                    for kt in range(nkt):
                        ins = t.matmul(OTP[qc % 2][0:1, QC:2 * QC], onesb,
                                       expT[qc % 4][:, kt * QC:(kt + 1) * QC],
                                       start=False, stop=(kt == nkt - 1),
                                       skip_group_check=True)
                    c_av.inc(ins)

                def emit_dmm(h, qc):
                    pass

                def emit_bcast(h, qc):
                    iq = NQC * h + qc
                    t.wait_ge(c_rec.sem, iq + 1)
                    if iq >= 1:
                        t.wait_ge(c_dcp.sem, iq)
                    ins = t.matmul(DNB, onesr, rrow[qc % 2][0:1, :],
                                   start=True, stop=True)
                    c_bcd.inc(ins)

                def emit_projm(h, m):
                    p = 3 * h + m
                    t.wait_ge(c_w[p % 3].sem, 16 * (p // 3 + 1))
                    for c in range(4):
                        k12 = 12 * h + 4 * m + c
                        if k12 >= 2:
                            t.wait_ge(c_qkv.sem, k12 - 1)
                        for dt in range(NDT):
                            ins = t.matmul(
                                PJ[k12 % 2],
                                W[p % 3][:, dt * 128:(dt + 1) * 128],
                                XB[:, dt * T + c * 512:
                                   dt * T + c * 512 + 512],
                                start=(dt == 0), stop=(dt == NDT - 1))
                        c_pj.inc(ins)

                for h in range(HH):
                    emit_projm(h, 0)
                    if h > 0:
                        emit_bcast(h - 1, NQC - 4)
                        emit_av(h - 1, NQC - 3)
                    emit_projm(h, 1)
                    if h > 0:
                        emit_bcast(h - 1, NQC - 3)
                        emit_av(h - 1, NQC - 2)
                    emit_projm(h, 2)
                    if h > 0:
                        emit_bcast(h - 1, NQC - 2)
                        emit_av(h - 1, NQC - 1)
                    for qc in range(NQC):
                        emit_scores(h, qc)
                        if qc == 0 and h > 0:
                            emit_bcast(h - 1, NQC - 1)
                        if qc >= 4:
                            emit_bcast(h, qc - 4)
                        if qc >= 3:
                            emit_av(h, qc - 3)
                emit_bcast(HH - 1, NQC - 4)
                emit_av(HH - 1, NQC - 3)
                emit_bcast(HH - 1, NQC - 3)
                emit_av(HH - 1, NQC - 2)
                emit_bcast(HH - 1, NQC - 2)
                emit_av(HH - 1, NQC - 1)
                emit_bcast(HH - 1, NQC - 1)

            @blk.scalar
            def _(s):
                def emit_dcp(iq):
                    s.wait_ge(c_bcd.sem, iq + 1)
                    if iq >= 1:
                        s.wait_ge(c_otv.sem, iq - 1)
                    c_dcp.inc(s.copy(dnb[iq % 2], DNB))

                for h in range(HH):
                    for m in range(3):
                        for c in range(4):
                            k12 = 12 * h + 4 * m + c
                            s.wait_ge(c_pj.sem, k12 + 1)
                            if m == 2 and c == 0 and h > 0:
                                for b in range(4):
                                    s.wait_ge(c_vt[b].sem, 64 * h)
                            dst = (Qb, Kb, Vb)[m]
                            ins = s.activation(
                                dst[:, c * 512:(c + 1) * 512], PJ[k12 % 2],
                                AF.Identity,
                                bias=bq_t[:, m * 32 + h: m * 32 + h + 1])
                            c_qkv.inc(ins)
                        if h > 0:
                            emit_dcp(NQC * (h - 1) + NQC - 4 + m)
                    if h > 0:
                        emit_dcp(NQC * (h - 1) + NQC - 1)
                    for qc in range(NQC):
                        iq = NQC * h + qc
                        if iq >= 4:
                            s.wait_ge(c_av.sem, iq - 3)
                        for gi in range(ngrp(qc)):
                            G = gbase(h, qc) + gi
                            s.wait_ge(c_scg.sem, G + 1)
                            ins = s.activation(
                                expT[qc % 4][:, 2 * gi * QC:
                                             2 * gi * QC + 2 * QC],
                                SC[G % 3], AF.Exp, scale=ISQ)
                            c_exp.inc(ins)
                        if qc >= 4:
                            emit_dcp(iq - 4)
                for qc in (NQC - 4, NQC - 3, NQC - 2, NQC - 1):
                    emit_dcp(NQC * (HH - 1) + qc)


            @blk.vector
            def _(v):
                def emit_norm(iq):
                    v.wait_ge(c_dcp.sem, iq + 1)
                    if iq % NQC == 0 and iq >= NQC:
                        v.wait_ge(c_osp.sem, 16 * (iq // NQC))
                    qc = iq % NQC
                    ins = v.tensor_tensor(OTh[:, qc * QC:(qc + 1) * QC],
                                          otu[iq % 2], dnb[iq % 2], ALU.mult)
                    _chain(v, c_vv, ins)
                    c_otv.inc(v.nop())

                def emit_copy(iq):
                    v.wait_ge(c_av.sem, iq + 1)
                    ins = v.tensor_copy(otu[iq % 2], OTP[iq % 2][:, 0:QC])
                    _chain(v, c_vv, ins)
                    c_otu.inc(v.nop())

                def emit_recip(iq):
                    v.wait_ge(c_av.sem, iq + 1)
                    ins = v.reciprocal(rrow[iq % 2][0:1, :],
                                       OTP[iq % 2][0:1, QC:2 * QC])
                    _chain(v, c_vv, ins)
                    c_rec.inc(v.nop())

                def vtail(h):
                    base = NQC * h
                    emit_norm(base + NQC - 5)
                    emit_copy(base + NQC - 3)
                    emit_recip(base + NQC - 3)
                    emit_norm(base + NQC - 4)
                    emit_copy(base + NQC - 2)
                    emit_recip(base + NQC - 2)
                    emit_norm(base + NQC - 3)
                    emit_copy(base + NQC - 1)
                    emit_recip(base + NQC - 1)
                    emit_norm(base + NQC - 2)
                    emit_norm(base + NQC - 1)

                for h in range(HH):
                    if h > 0:
                        vtail(h - 1)
                    for j in range(NQC):
                        iq = NQC * h + j
                        if j >= 5:
                            emit_norm(iq - 5)
                        if j >= 3:
                            emit_copy(iq - 3)
                            emit_recip(iq - 3)
                vtail(HH - 1)

            @blk.sync
            def _(sy):
                for h in range(HH):
                    if h >= 2:
                        sy.wait_ge(c_av.sem, NQC * (h - 1))
                    for c in range(4):
                        sy.wait_ge(c_qkv.sem, 12 * h + 8 + c + 1)
                        for j in range(4):
                            kt = 4 * c + j
                            ins = sy.dma_start_transpose(
                                Vtok[h % 2][:, kt * 128:(kt + 1) * 128],
                                Vb[:, kt * 128:(kt + 1) * 128])
                            c_vt[c].inc(ins, 16)
                    sy.wait_ge(c_otv.sem, NQC * (h + 1))
                    c_osp.inc(sy.dma_start(out=otspill[h], in_=OTh), 16)
                sy.wait_ge(c_osp.sem, 16 * HH)

        if stop == "p3a":
            with ExitStack() as st, nc.Block() as blk:
                c_otl = Ctr(gst, nc, "otl")

                @blk.gpsimd
                def _(g):
                    hh = _DEV_NH or NH
                    for h in range(hh):
                        c_otl.inc(g.dma_start(out=XB[:, h * T:(h + 1) * T],
                                              in_=otspill[h]), 16)
                    g.wait_ge(c_otl.sem, 16 * hh)
            _emit_dump_xb(nc, XB, out, c_ost)
            return nc

        with ExitStack() as st, nc.Block() as blk:
            slot = [st.enter_context(nc.sbuf_tensor(
                f"wos{i}", [128, NDT * 128], BF16))[:] for i in range(2)]
            x1r = [st.enter_context(nc.sbuf_tensor(
                f"x1r{i}", [128, T], BF16))[:] for i in range(2)]
            x2a = [st.enter_context(nc.sbuf_tensor(
                f"x2a{i}", [128, 512], BF16))[:] for i in range(4)]
            x2b = [st.enter_context(nc.sbuf_tensor(
                f"x2b{i}", [128, 512], BF16))[:] for i in range(4)]
            APs = [st.enter_context(nc.psum_tensor(
                f"APp{i}", [128, 512], F32))[:] for i in range(2)]
            c_otl = Ctr(gst, nc, "otl")
            c_wol = [Ctr(gst, nc, f"wol{i}") for i in range(2)]
            c_x1l = [Ctr(gst, nc, f"x1l{i}") for i in range(2)]
            c_pj3 = Ctr(gst, nc, "pj3")
            c_x2a = Ctr(gst, nc, "x2a")
            c_x2b = Ctr(gst, nc, "x2b")
            c_x2s = [Ctr(gst, nc, f"x2s{i}") for i in range(4)]

            @blk.gpsimd
            def _(g):
                for h in range(NH):
                    c_otl.inc(g.dma_start(out=XB[:, h * T:(h + 1) * T],
                                          in_=otspill[h]), 16)
                for ot in range(NDT):
                    if ot >= 2:
                        g.wait_ge(c_pj3.sem, 4 * (ot - 1))
                    c_wol[ot % 2].inc(g.dma_start(out=slot[ot % 2],
                                                  in_=wo[ot]), 16)
                    if ot >= 2:
                        g.wait_ge(c_x2b.sem, 4 * (ot - 1))
                    c_x1l[ot % 2].inc(g.dma_start(out=x1r[ot % 2],
                                                  in_=xspill[ot]), 16)

            @blk.tensor
            def _(t):
                t.wait_ge(c_otl.sem, 16 * NH)
                for ot in range(NDT):
                    t.wait_ge(c_wol[ot % 2].sem, 16 * (ot // 2 + 1))
                    for c in range(4):
                        k = 4 * ot + c
                        if k >= 2:
                            t.wait_ge(c_x2a.sem, k - 1)
                        for dt in range(NDT):
                            ins = t.matmul(
                                APs[k % 2],
                                slot[ot % 2][:, dt * 128:(dt + 1) * 128],
                                XB[:, dt * T + c * 512:
                                   dt * T + c * 512 + 512],
                                start=(dt == 0), stop=(dt == NDT - 1))
                        c_pj3.inc(ins)

            @blk.scalar
            def _(s):
                for ot in range(NDT):
                    for c in range(4):
                        k = 4 * ot + c
                        s.wait_ge(c_pj3.sem, k + 1)
                        if k >= 4:
                            s.wait_ge(c_x2b.sem, k - 3)
                        ins = s.activation(x2a[k % 4], APs[k % 2],
                                           AF.Identity,
                                           bias=bo_t[:, ot:ot + 1])
                        c_x2a.inc(ins)

            @blk.vector
            def _(v):
                for ot in range(NDT):
                    v.wait_ge(c_x1l[ot % 2].sem, 16 * (ot // 2 + 1))
                    for c in range(4):
                        k = 4 * ot + c
                        v.wait_ge(c_x2a.sem, k + 1)
                        if k >= 4:
                            v.wait_ge(c_x2s[k % 4].sem, 16 * (k // 4))
                        ins = v.tensor_tensor(
                            x2b[k % 4], x2a[k % 4],
                            x1r[ot % 2][:, c * 512:(c + 1) * 512],
                            ALU.add)
                        c_x2b.inc(ins)

            @blk.sync
            def _(sy):
                for ot in range(NDT):
                    for c in range(4):
                        k = 4 * ot + c
                        sy.wait_ge(c_x2b.sem, k + 1)
                        ins = sy.dma_start(
                            out=x2spill[ot][:, c * 512:(c + 1) * 512],
                            in_=x2b[k % 4])
                        c_x2s[k % 4].inc(ins, 16)
                for i in range(4):
                    sy.wait_ge(c_x2s[i].sem, 16 * 32)

        if stop == "p3":
            with ExitStack() as st, nc.Block() as blk:
                c_rl = Ctr(gst, nc, "rl3")

                @blk.gpsimd
                def _(g):
                    for dt in range(NDT):
                        c_rl.inc(g.dma_start(out=XB[:, dt * T:(dt + 1) * T],
                                             in_=x2spill[dt]), 16)
                    g.wait_ge(c_rl.sem, 16 * NDT)
            _emit_dump_xb(nc, XB, out, c_ost)
            return nc

        with ExitStack() as st, nc.Block() as blk:
            bufs = mk_ffn_bufs(st, nc, True, shared_sb, "z4")
            c_ab4 = Ctr(gst, nc, "ab4")
            c_x2g = [Ctr(gst, nc, f"x2g{g}") for g in range(4)]

            @blk.gpsimd
            def _(g):
                c_ab4.inc(g.dma_start(out=bufs["ART"], in_=art), 16)
                c_ab4.inc(g.dma_start(out=bufs["BT"], in_=bt), 16)
                for gi in range(4):
                    for dt in range(8 * gi, 8 * gi + 8):
                        c_x2g[gi].inc(g.dma_start(
                            out=XB[:, dt * T:(dt + 1) * T],
                            in_=x2spill[dt]), 16)

            def gate4(eng, dt):
                if dt == 0:
                    eng.wait_ge(c_ab4.sem, 32)
                if dt % 8 == 0:
                    eng.wait_ge(c_x2g[dt // 8].sem, 128)

            emit_stats(nc, gst, blk, XB, bufs, ctrs, gate4, "f")
            emit_ffn(nc, gst, blk, XB, bufs, ctrs, gate4, "f", out, c_ost)

    return nc


def _emit_dump_xb(nc, XB, out, c_ost):
    with ExitStack() as st, nc.Block() as blk:
        fo = [st.enter_context(nc.sbuf_tensor(
            f"dump{i}", [128, T], F32))[:] for i in range(2)]
        c_dv = Ctr(st, nc, "dumpv")
        ost0 = [c.n for c in c_ost]

        @blk.vector
        def _(v):
            for dt in range(NDT):
                if dt >= 2:
                    v.wait_ge(c_ost[dt % 2].sem,
                              ost0[dt % 2] + 16 * (dt // 2))
                ins = v.tensor_copy(fo[dt % 2], XB[:, dt * T:(dt + 1) * T])
                c_dv.inc(ins)

        @blk.sync
        def _(sy):
            for dt in range(NDT):
                sy.wait_ge(c_dv.sem, dt + 1)
                c_ost[dt % 2].inc(
                    sy.dma_start(out=out[dt * 128:(dt + 1) * 128, :],
                                 in_=fo[dt % 2]), 16)
            sy.wait_ge(c_ost[0].sem, ost0[0] + 16 * 16)
            sy.wait_ge(c_ost[1].sem, ost0[1] + 16 * 16)


def kernel(**inputs):
    xT, shared = host_pack(inputs)
    nc = build_nc(stop=_STOP_PHASE)
    in_maps = []
    for core in range(8):
        m = {"xT": np.ascontiguousarray(xT[core])}
        m.update(shared)
        in_maps.append(m)
    res = run_bass_kernel_spmd(nc, in_maps, core_ids=list(range(8)))
    outs = [np.asarray(res.results[i]["out"], np.float32) for i in range(8)]
    return np.ascontiguousarray(np.stack(outs, 0).transpose(0, 2, 1))
